# version 07
# baseline (speedup 1.0000x reference)
import numpy as np
import contextlib
import concourse.bass as bass
import concourse.mybir as mybir
from concourse.bass_utils import run_bass_kernel_spmd

F32 = mybir.dt.float32
ALU = mybir.AluOpType
ACTF = mybir.ActivationFunctionType
AXX = mybir.AxisListType.X

L, DM, DI, S, E, HD = 4096, 1024, 2048, 16, 8, 512
TC = 512
NCH = L // TC
NKC = DM // 128
NCC = DI // 128
NJT = 4 * DM // 128
EPS = 1e-5

ENGS = ("sync", "scalar", "vector", "tensor", "gpsimd")
DEBUG = False


class Bld:
    def __init__(self, nc):
        self.nc = nc
        self.qs = {e: [] for e in ENGS}
        self.sh = {}
        self.cnt = {}
        self.cap = None
        self.alias = {}

    def begin_capture(self, engines, alias=None):
        self.cap = set(engines)
        self.capm = []
        self.alias = alias or {}

    def end_capture(self):
        c = self.capm
        self.cap = None
        self.capm = None
        self.alias = {}
        return c

    def drain(self, caplist, n=None):
        k = len(caplist) if n is None else min(n, len(caplist))
        for (eng, f) in caplist[:k]:
            self.qs[eng].append(f)
        del caplist[:k]

    def new_sem(self, name, stack):
        h = stack.enter_context(self.nc.semaphore(name))
        self.sh[name] = h
        self.cnt[name] = 0

    def tok(self, name):
        name = self.alias.get(name, name)
        return (name, self.cnt[name])

    def op(self, eng, build, waits=(), inc=None, inck=1):
        waits = [t for t in waits if t is not None]
        if inc is not None:
            inc = self.alias.get(inc, inc)
            self.cnt[inc] += inck
        sh = self.sh
        incv = inc

        def f(e):
            for (s, v) in waits:
                e.wait_ge(sh[s], v)
            ins = build(e)
            if incv is not None:
                ins.then_inc(sh[incv], inck)
        if self.cap is not None and eng in self.cap:
            self.capm.append((eng, f))
        else:
            self.qs[eng].append(f)
        return self.tok(inc) if inc is not None else None

    def waits(self, eng, waits):
        waits = [t for t in waits if t is not None]
        sh = self.sh

        def f(e):
            for (s, v) in waits:
                e.wait_ge(sh[s], v)
        self.qs[eng].append(f)


class Arena:
    def __init__(self, total):
        self.total = total
        self.off = 0

    def alloc(self, n, name=""):
        assert self.off + n <= self.total, f"arena overflow at {name}: {self.off}+{n}"
        o = self.off
        self.off += n
        return o


def build_program():
    nc = bass.Bass("TRN2", target_bir_lowering=False, detect_race_conditions=False)
    DT = F32

    x_in = nc.dram_tensor("x", [L, DM], DT, kind="ExternalInput")
    w_inp = nc.dram_tensor("in_proj_w", [DM, 4 * DM], DT, kind="ExternalInput")
    w_conv = nc.dram_tensor("conv_w", [DI, 1, 4], DT, kind="ExternalInput")
    b_conv = nc.dram_tensor("conv_b", [DI], DT, kind="ExternalInput")
    w_xp = nc.dram_tensor("x_proj_w", [DI, 33], DT, kind="ExternalInput")
    dt_w = nc.dram_tensor("dt_w", [DI], DT, kind="ExternalInput")
    dt_b = nc.dram_tensor("dt_b", [DI], DT, kind="ExternalInput")
    a_log = nc.dram_tensor("A_log", [DI, S], DT, kind="ExternalInput")
    d_vec = nc.dram_tensor("D", [DI], DT, kind="ExternalInput")
    w_outp = nc.dram_tensor("out_proj_w", [DI, DM], DT, kind="ExternalInput")
    w_rt = nc.dram_tensor("routing_W", [E, DM], DT, kind="ExternalInput")
    b_e = nc.dram_tensor("expert_bias", [E], DT, kind="ExternalInput")
    sw1 = nc.dram_tensor("sW1", [DM, HD], DT, kind="ExternalInput")
    sb1 = nc.dram_tensor("sb1", [HD], DT, kind="ExternalInput")
    sw2 = nc.dram_tensor("sW2", [HD, DM], DT, kind="ExternalInput")
    sb2 = nc.dram_tensor("sb2", [DM], DT, kind="ExternalInput")
    rw1 = nc.dram_tensor("rW1", [E, DM, HD], DT, kind="ExternalInput")
    rb1 = nc.dram_tensor("rb1", [E, HD], DT, kind="ExternalInput")
    rw2 = nc.dram_tensor("rW2", [E, HD, DM], DT, kind="ExternalInput")
    rb2 = nc.dram_tensor("rb2", [E, DM], DT, kind="ExternalInput")
    c_in = nc.dram_tensor("c_vec", [DM], DT, kind="ExternalInput")
    w_mod = nc.dram_tensor("wmod", [DM, 6 * DM], DT, kind="ExternalInput")
    eye_in = nc.dram_tensor("eye", [128, 128], DT, kind="ExternalInput")
    ones_in = nc.dram_tensor("onesrow", [1, 128], DT, kind="ExternalInput")

    x_mid = nc.dram_tensor("x_mid", [DM, L], DT)
    sz_sp = nc.dram_tensor("sz_sp", [DI, TC], DT)
    out = nc.dram_tensor("out", [L, DM], DT, kind="ExternalOutput")

    AF = 53200
    stack = contextlib.ExitStack()
    ar = stack.enter_context(nc.sbuf_tensor("arena", [128, AF], DT))
    pb = [stack.enter_context(nc.psum_tensor(f"pb{i}", [128, 512], F32)) for i in range(8)]

    b = Bld(nc)
    for s in ("dw", "dx", "dg", "tns", "act", "dve", "st", "bc", "fdv", "fac"):
        b.new_sem(s, stack)

    A = Arena(AF)
    o_eye = A.alloc(128, "eye")
    o_ones = A.alloc(128, "ones_row")
    o_onescol = A.alloc(1, "ones_col")
    o_mod = A.alloc(64, "mod")
    o_ndtw = A.alloc(16, "ndtw")
    o_ndtb = A.alloc(16, "ndtb")
    o_nega = A.alloc(256, "negA")
    o_convw = A.alloc(64, "convw")
    o_convb = A.alloc(16, "convb")
    o_dvec = A.alloc(16, "Dvec")
    o_xpw = A.alloc(528, "xpw")
    o_state = A.alloc(256, "state")
    o_rt = A.alloc(64, "rtw")
    o_eb = A.alloc(1, "eb")
    o_rb1 = A.alloc(36, "rb1")
    o_sb2 = A.alloc(8, "sb2cols")
    o_tails = A.alloc(48, "tails")
    o_rows = A.alloc(3 * 512, "rows")
    o_bcdt = A.alloc(512, "bcdt")
    o_scr2 = A.alloc(512, "scr2")
    o_R1 = A.alloc(512, "R1")
    o_R2 = A.alloc(512, "R2")
    o_dAr = A.alloc(2 * 512, "dA_ring")
    o_dBr = A.alloc(2 * 512, "dB_ring")
    o_Bbc = A.alloc(2 * 512, "Bbc")
    o_Cbc = A.alloc(2 * 512, "Cbc")
    o_lgs = A.alloc(64, "lgscratch")
    o_rb2s = A.alloc(128, "rb2stat")
    o_fex = A.alloc(5 * 512, "acc_extra")
    o_xT = A.alloc(8 * 512, "xT")
    o_hT = A.alloc(8 * 512, "hT")
    o_big = A.alloc(16 * 515, "big")
    o_xc = A.alloc(16 * 512, "xc")
    o_sz = A.alloc(16 * 512, "sz")
    o_mdt = A.alloc(8 * 512, "mdt")
    o_wring = A.alloc(2 * 2048, "wring")
    assert A.off <= AF, A.off

    o_dtxm = o_xc
    o_Xh = o_sz
    o_hdn = o_sz + 4096
    o_cfb = o_sz + 6144
    o_accB = o_sz + 6656
    o_outst = o_sz
    o_xmres = o_sz + 1024
    o_fsqA = o_sz + 4096
    o_fsqB = o_sz + 4608
    o_fr0 = o_sz + 5120
    o_fr1 = o_sz + 5632
    o_fr2 = o_sz + 6144
    o_fR1 = o_sz + 6656
    o_fR2 = o_sz + 7168
    o_w1r = o_wring
    o_w2r = o_wring + 2048
    o_lg = o_scr2
    o_cf = o_R2
    o_hr = o_wring + 3072

    def acc_off(ot):
        return (o_fex + ot * 512) if ot < 5 else (o_accB + (ot - 5) * 512)

    def arp(off, nf, p0=0, np_=128):
        return ar[p0:p0 + np_, off:off + nf]

    def dma(eng, sem, dst, src, waits=(), slow=False):
        return b.op(eng, lambda e, d=dst, s=src, sl=slow: e.dma_start(
            out=d, in_=s, allow_slow_non_contiguous=sl),
            waits=waits, inc=sem, inck=16)

    t_eye = dma("sync", "dw", arp(o_eye, 128), eye_in.ap())
    dma("sync", "dw", arp(o_ones, 128, 0, 1), ones_in.ap())

    def colmajor_src(t, n):
        return bass.AP(t, 0, [[1, 128], [128, n], [1, 1]])

    def colmajor_dst(off, n):
        return bass.AP(ar, off, [[AF, 128], [1, n], [1, 1]])

    dma("sync", "dw", colmajor_dst(o_ndtw, 16), colmajor_src(dt_w, 16), slow=True)
    dma("sync", "dw", colmajor_dst(o_ndtb, 16), colmajor_src(dt_b, 16), slow=True)
    dma("sync", "dw", colmajor_dst(o_convb, 16), colmajor_src(b_conv, 16), slow=True)
    dma("sync", "dw", colmajor_dst(o_dvec, 16), colmajor_src(d_vec, 16), slow=True)
    dma("sync", "dw", bass.AP(ar, o_nega, [[AF, 128], [16, 16], [1, 16]]),
        bass.AP(a_log, 0, [[16, 128], [128 * 16, 16], [1, 16]]))
    dma("sync", "dw", bass.AP(ar, o_convw, [[AF, 128], [4, 16], [1, 4]]),
        bass.AP(w_conv, 0, [[4, 128], [128 * 4, 16], [1, 4]]))
    t_xp = dma("sync", "dw", bass.AP(ar, o_xpw, [[AF, 128], [33, 16], [1, 33]]),
               bass.AP(w_xp, 0, [[33, 128], [128 * 33, 16], [1, 33]]))
    for kc in range(NKC):
        t_rt = dma("sync", "dw", bass.AP(ar, o_rt + kc * 8, [[AF, 128], [1, 8]]),
                   bass.AP(w_rt, kc * 128, [[1, 128], [DM, 8]]), slow=True)
    t_ebi = dma("sync", "dw", arp(o_eb, 1, 0, 8), bass.AP(b_e, 0, [[1, 8], [1, 1]]), slow=True)
    dma("sync", "dw", bass.AP(ar, o_rb1, [[AF, 128], [4, 8], [1, 4]]),
        bass.AP(rb1, 0, [[1, 128], [HD, 8], [128, 4]]), slow=True)
    dma("sync", "dw", bass.AP(ar, o_rb1 + 32, [[AF, 128], [1, 4], [1, 1]]),
        bass.AP(sb1, 0, [[1, 128], [128, 4], [1, 1]]), slow=True)
    dma("sync", "dw", colmajor_dst(o_sb2, 8), colmajor_src(sb2, 8), slow=True)
    dma("sync", "dw", colmajor_dst(o_mod + 56, 8), colmajor_src(c_in, 8), slow=True)
    tw0 = b.tok("dw")

    b.op("vector", lambda e: e.memset(arp(o_onescol, 1), 1.0), waits=[tw0])
    b.op("vector", lambda e: e.tensor_scalar(arp(o_ndtw, 16), arp(o_ndtw, 16), -1.0, None, ALU.mult))
    b.op("vector", lambda e: e.tensor_scalar(arp(o_ndtb, 16), arp(o_ndtb, 16), -1.0, None, ALU.mult))
    b.op("vector", lambda e: e.tensor_scalar(arp(o_nega, 256), arp(o_nega, 256), 6.0, None, ALU.min))
    b.op("vector", lambda e: e.tensor_scalar(arp(o_nega, 256), arp(o_nega, 256), -6.0, None, ALU.max))
    b.op("vector", lambda e: e.memset(arp(o_state, 256), 0.0))
    b.op("vector", lambda e: e.memset(arp(o_tails, 48), 0.0))
    b.op("vector", lambda e: e.memset(arp(o_lgs + 50, 1), 0.0), inc="dve")
    tconst = b.tok("dve")
    b.op("scalar", lambda e: e.activation(arp(o_nega, 256), arp(o_nega, 256), ACTF.Exp),
         waits=[tconst])
    b.op("scalar", lambda e: e.activation(arp(o_mod + 56, 8), arp(o_mod + 56, 8), ACTF.Silu), inc="act")
    tsc = b.tok("act")

    mm_toks = [None, None]
    ev_toks = [None, None]
    for mt in range(48):
        woff = o_wring + (mt % 2) * 2048
        wt = [mm_toks[mt % 2]] if mt >= 2 else []
        tok_w = b.op("sync", lambda e, mo=mt, wo=woff: e.dma_start(
            out=bass.AP(ar, wo, [[AF, 128], [128, 8], [1, 128]]),
            in_=bass.AP(w_mod, mo * 128, [[6 * DM, 128], [128 * 6 * DM, 8], [1, 128]])),
            waits=wt, inc="dw", inck=16)
        awt = [ev_toks[mt % 2]] if mt >= 2 else []
        for kc in range(NKC):
            b.op("tensor", lambda e, wo=woff, kc=kc, mt=mt: e.matmul(
                pb[mt % 2][:, 0:1], arp(wo + kc * 128, 128), arp(o_mod + 56 + kc, 1),
                start=(kc == 0), stop=(kc == NKC - 1)),
                waits=(([tok_w, tsc] + awt) if kc == 0 else []),
                inc=("tns" if kc == NKC - 1 else None))
        mm_toks[mt % 2] = b.tok("tns")
        b.op("scalar", lambda e, mt=mt: e.copy(arp(o_mod + mt, 1), pb[mt % 2][:, 0:1]),
             waits=[mm_toks[mt % 2]], inc="act")
        ev_toks[mt % 2] = b.tok("act")
    t_mod_mm = b.tok("tns")
    tmod = b.tok("act")
    b.op("vector", lambda e: e.tensor_scalar(arp(o_mod + 8, 8), arp(o_mod + 8, 8), 1.0, None, ALU.add),
         waits=[tmod], inc="dve")
    b.op("vector", lambda e: e.tensor_scalar(arp(o_mod + 32, 8), arp(o_mod + 32, 8), 1.0, None, ALU.add), inc="dve")
    for base in (16, 40):
        b.op("scalar", lambda e, ba=base: e.activation(arp(o_mod + 48, 8), arp(o_mod + ba, 8), ACTF.Abs),
             waits=[tmod, b.tok("dve")], inc="act")
        b.op("vector", lambda e: e.tensor_scalar(arp(o_mod + 48, 8), arp(o_mod + 48, 8), 0.25, None, ALU.max),
             waits=[b.tok("act")], inc="dve")
        b.op("vector", lambda e, ba=base: e.tensor_scalar(arp(o_mod + 56, 8), arp(o_mod + ba, 8), 0.0, None, ALU.is_ge),
             waits=[b.tok("dve")], inc="dve")
        b.op("vector", lambda e: e.tensor_scalar(arp(o_mod + 56, 8), arp(o_mod + 56, 8), 2.0, None, ALU.mult),
             waits=[b.tok("dve")], inc="dve")
        b.op("vector", lambda e: e.tensor_scalar(arp(o_mod + 56, 8), arp(o_mod + 56, 8), -1.0, None, ALU.add),
             waits=[b.tok("dve")], inc="dve")
        b.op("vector", lambda e, ba=base: e.tensor_tensor(arp(o_mod + ba, 8), arp(o_mod + 48, 8), arp(o_mod + 56, 8), ALU.mult),
             waits=[b.tok("dve")], inc="dve")
    tmodv = b.tok("dve")

    MOD = dict(sa=0, ca1=8, ga=16, sb=24, cb1=32, gb=40)

    def modcol(name, j):
        return arp(o_mod + MOD[name] + j, 1)

    def emit_ln(src_off, dst_off, scale_name, shift_name, waits_first,
                sqA, sqB, r0, r1, r2, R1o, R2o):
        for kc in range(NKC):
            wt = list(waits_first) if kc == 0 else []
            b.op("tensor", lambda e, kc=kc: e.matmul(
                pb[6][0:1, :], arp(o_onescol, 1), arp(src_off + kc * 512, 512),
                start=(kc == 0), stop=(kc == NKC - 1)),
                waits=wt, inc=("tns" if kc == NKC - 1 else None))
        tsum = b.tok("tns")
        sqtok = [None, None]
        for kc in range(NKC):
            so = sqA if kc % 2 == 0 else sqB
            wt = list(waits_first) if kc == 0 else []
            if kc >= 2:
                wt.append(sqtok[kc % 2])
            b.op("scalar", lambda e, kc=kc, so=so: e.activation(
                arp(so, 512), arp(src_off + kc * 512, 512), ACTF.Square),
                waits=wt, inc="act")
            ta = b.tok("act")
            b.op("tensor", lambda e, kc=kc, so=so: e.matmul(
                pb[7][0:1, :], arp(o_onescol, 1), arp(so, 512),
                start=(kc == 0), stop=(kc == NKC - 1)),
                waits=[ta], inc="tns")
            sqtok[kc % 2] = b.tok("tns")
        tsq = b.tok("tns")
        b.op("vector", lambda e: e.tensor_scalar(arp(r0, 512, 0, 1), pb[6][0:1, :], 1.0 / DM, None, ALU.mult),
             waits=[tsum, tsq])
        b.op("vector", lambda e: e.tensor_scalar(arp(r1, 512, 0, 1), pb[7][0:1, :], 1.0 / DM, None, ALU.mult))
        b.op("vector", lambda e: e.tensor_tensor(arp(r2, 512, 0, 1), arp(r0, 512, 0, 1), arp(r0, 512, 0, 1), ALU.mult))
        b.op("vector", lambda e: e.tensor_tensor(arp(r1, 512, 0, 1), arp(r1, 512, 0, 1), arp(r2, 512, 0, 1), ALU.subtract))
        b.op("vector", lambda e: e.tensor_scalar(arp(r1, 512, 0, 1), arp(r1, 512, 0, 1), EPS, None, ALU.add), inc="dve")
        tv = b.tok("dve")
        b.op("scalar", lambda e: e.activation(arp(r1, 512, 0, 1), arp(r1, 512, 0, 1), ACTF.Ln), waits=[tv])
        b.op("scalar", lambda e: e.activation(arp(r1, 512, 0, 1), arp(r1, 512, 0, 1), ACTF.Exp, scale=-0.5), inc="act")
        ti = b.tok("act")
        b.op("vector", lambda e: e.tensor_tensor(arp(r2, 512, 0, 1), arp(r0, 512, 0, 1), arp(r1, 512, 0, 1), ALU.mult),
             waits=[ti])
        b.op("vector", lambda e: e.tensor_scalar(arp(r2, 512, 0, 1), arp(r2, 512, 0, 1), -1.0, None, ALU.mult), inc="dve")
        trows = b.tok("dve")
        b.op("tensor", lambda e: e.matmul(pb[6][:, :], arp(o_ones, 128, 0, 1),
                                          arp(r1, 512, 0, 1), start=True, stop=True), waits=[trows])
        b.op("tensor", lambda e: e.matmul(pb[7][:, :], arp(o_ones, 128, 0, 1),
                                          arp(r2, 512, 0, 1), start=True, stop=True), inc="tns")
        tbc = b.tok("tns")
        b.op("scalar", lambda e: e.copy(arp(R1o, 512), pb[6][:, :]), waits=[tbc])
        b.op("scalar", lambda e: e.copy(arp(R2o, 512), pb[7][:, :]), inc="act")
        tR = b.tok("act")
        for kc in range(NKC):
            wt = [tR] if kc == 0 else []
            b.op("vector", lambda e, kc=kc: e.tensor_tensor(
                arp(dst_off + kc * 512, 512), arp(src_off + kc * 512, 512), arp(R1o, 512), ALU.mult), waits=wt)
            b.op("vector", lambda e, kc=kc: e.tensor_tensor(
                arp(dst_off + kc * 512, 512), arp(dst_off + kc * 512, 512), arp(R2o, 512), ALU.add))
            b.op("vector", lambda e, kc=kc, sn=scale_name, shn=shift_name: e.tensor_scalar(
                arp(dst_off + kc * 512, 512), arp(dst_off + kc * 512, 512),
                modcol(sn, kc), modcol(shn, kc), ALU.mult, ALU.add), inc="dve")
        return b.tok("dve")

    fst = {"rb2_last": None, "w1_last": None, "w2_last": [None, None],
           "w1dma_last": [None, None], "acc_free": None, "thdn_dead": None,
           "xh_dead": None, "store_last": None}

    def emit_F(fc, t_spill, t_xmst, t_inproj_mm, acc_free_tok, bchook):
        b.begin_capture(["vector", "scalar"], alias={"dve": "fdv", "act": "fac"})
        tok_x = b.op("gpsimd", lambda e, fc=fc: e.dma_start(
            out=bass.AP(ar, o_Xh, [[AF, 128], [512, 8], [1, TC]]),
            in_=bass.AP(x_mid, fc * TC, [[L, 128], [128 * L, 8], [1, TC]])),
            waits=[t_spill, t_xmst], inc="dg", inck=16)
        tln2 = emit_ln(o_Xh, o_Xh, "cb1", "sb", [tok_x, acc_free_tok],
                       o_fsqA, o_fsqB, o_fr0, o_fr1, o_fr2, o_fR1, o_fR2)
        for kc in range(NKC):
            wt12 = [tln2, t_rt, t_ebi] if kc == 0 else []
            b.op("tensor", lambda e, kc=kc: e.matmul(
                pb[6][0:8, :], arp(o_rt + kc * 8, 8), arp(o_Xh + kc * 512, 512),
                start=(kc == 0), stop=(kc == NKC - 1)),
                waits=wt12, inc=("tns" if kc == NKC - 1 else None))
        trt = b.tok("tns")
        b.op("vector", lambda e: e.tensor_scalar(arp(o_lg, 512, 0, 8), pb[6][0:8, :],
                                                 arp(o_eb, 1, 0, 8), None, ALU.add),
             waits=[trt], inc="dve")
        tlg = b.tok("dve")
        for tt in range(4):
            b.op("tensor", lambda e, tt=tt: e.transpose(
                pb[7][:, tt * 8:(tt + 1) * 8],
                arp(o_lg + tt * 128, 128, 0, 8),
                arp(o_eye, 8, 0, 8)),
                waits=([tlg, t_eye] if tt == 0 else []), inc=("tns" if tt == 3 else None))
        tlgT = b.tok("tns")
        b.op("scalar", lambda e: e.copy(arp(o_lgs, 32), pb[7][:, 0:32]), waits=[tlgT], inc="act")
        tlgT2 = b.tok("act")
        for tt in range(4):
            lo = o_lgs + tt * 8
            sA, sB, sm1, sm2 = o_lgs + 32, o_lgs + 40, o_lgs + 48, o_lgs + 49
            wt13 = [tlgT2] if tt == 0 else []
            b.op("vector", lambda e, lo=lo, sm1=sm1: e.tensor_reduce(
                arp(sm1, 1), arp(lo, 8), AXX, ALU.max), waits=wt13, inc="dve")
            b.op("vector", lambda e, lo=lo, sA=sA, sm1=sm1: e.tensor_scalar(
                arp(sA, 8), arp(lo, 8), arp(sm1, 1), None, ALU.is_equal),
                waits=[b.tok("dve")], inc="dve")
            b.op("vector", lambda e, lo=lo, sA=sA: e.scalar_tensor_tensor(
                arp(sA, 8), arp(sA, 8), -1e30, arp(lo, 8), ALU.mult, ALU.add),
                waits=[b.tok("dve")], inc="dve")
            b.op("vector", lambda e, sA=sA, sm2=sm2: e.tensor_reduce(
                arp(sm2, 1), arp(sA, 8), AXX, ALU.max),
                waits=[b.tok("dve")], inc="dve")
            b.op("vector", lambda e, lo=lo, sA=sA, sm2=sm2: e.tensor_scalar(
                arp(sA, 8), arp(lo, 8), arp(sm2, 1), None, ALU.is_ge),
                waits=[b.tok("dve")], inc="dve")
            tge = b.tok("dve")
            b.op("scalar", lambda e, lo=lo, sB=sB: e.activation(
                arp(sB, 8), arp(lo, 8), ACTF.Sigmoid, scale=-1.0), waits=[tge], inc="act")
            b.op("scalar", lambda e, sB=sB: e.activation(arp(sB, 8), arp(sB, 8), ACTF.Ln),
                 waits=[b.tok("act")], inc="act")
            b.op("scalar", lambda e, sB=sB: e.activation(arp(sB, 8), arp(sB, 8), ACTF.Ln, scale=-1.0),
                 waits=[b.tok("act")], inc="act")
            b.op("scalar", lambda e, sB=sB: e.activation(arp(sB, 8), arp(sB, 8), ACTF.Exp, scale=0.5),
                 waits=[b.tok("act")], inc="act")
            tact = b.tok("act")
            b.op("vector", lambda e, lo=lo, sA=sA, sB=sB: e.tensor_tensor(
                arp(lo, 8), arp(sB, 8), arp(sA, 8), ALU.mult), waits=[tact], inc="dve")
        tcfT = b.tok("dve")
        for tt in range(4):
            b.op("tensor", lambda e, tt=tt: e.transpose(
                pb[7][0:8, tt * 128:(tt + 1) * 128],
                arp(o_lgs + tt * 8, 8),
                arp(o_eye, 128)),
                waits=([tcfT] if tt == 0 else []), inc=("tns" if tt == 3 else None))
        tcmb = b.tok("tns")
        b.op("scalar", lambda e: e.copy(arp(o_cf, 512, 0, 8), pb[7][0:8, :]), waits=[tcmb], inc="act")
        tcf = b.tok("act")
        caps_front = b.end_capture()
        b.begin_capture(["vector", "scalar"], alias={"dve": "fdv", "act": "fac"})
        bchook()

        w1_toks = [None, None]
        w2_toks = [None, None]
        w1_ev = [None, None]
        rb2_ev = [None, None]
        hdn_free = None
        cfb_ev = None
        cfb_cons = None
        cfb_mm_prev = None
        acc_tok = [None] * NKC
        marks = []
        for e_i in range(E + 1):
            bchook(3)
            if e_i < E:
                trow = b.op("gpsimd", lambda e, e_i=e_i: e.dma_start(
                    out=arp(o_scr2, 512, 0, 1),
                    in_=ar[e_i:e_i + 1, o_cf:o_cf + 512]),
                    waits=[tcf, cfb_mm_prev], inc="dg", inck=16)
                b.op("tensor", lambda e: e.matmul(
                    pb[6][:, :], arp(o_ones, 128, 0, 1),
                    arp(o_scr2, 512, 0, 1), start=True, stop=True),
                    waits=[trow, cfb_ev], inc="tns")
                tcb = b.tok("tns")
                cfb_mm_prev = tcb
                b.op("scalar", lambda e: e.copy(arp(o_cfb, 512), pb[6][:, :]),
                     waits=[tcb, cfb_cons], inc="act")
                cfb_ev = b.tok("act")
            for ht in range(4):
                wslot = o_w1r + (ht % 2) * 1024
                if e_i == 0 and ht < 2:
                    wtd = [t_inproj_mm]
                else:
                    wtd = [w1_toks[ht % 2]]
                if e_i < E:
                    srcw = bass.AP(rw1, e_i * DM * HD + ht * 128, [[HD, 128], [128 * HD, 8], [1, 128]])
                else:
                    srcw = bass.AP(sw1, ht * 128, [[HD, 128], [128 * HD, 8], [1, 128]])
                weng = "sync" if ht % 2 == 0 else "gpsimd"
                tok_w1 = b.op(weng, lambda e, wo=wslot, src=srcw: e.dma_start(
                    out=bass.AP(ar, wo, [[AF, 128], [128, 8], [1, 128]]), in_=src),
                    waits=wtd, inc=("dw" if ht % 2 == 0 else "dg"), inck=16)
                bk = 2 + (ht % 2)
                if ht < 2:
                    wt15 = [tok_w1, tln2] if e_i == 0 else [tok_w1, acc_tok[6 + ht]]
                else:
                    wt15 = [tok_w1, w1_ev[ht % 2]]
                for kc in range(NKC):
                    b.op("tensor", lambda e, wo=wslot, kc=kc, bk=bk: e.matmul(
                        pb[bk][:, :], arp(wo + kc * 128, 128), arp(o_Xh + kc * 512, 512),
                        start=(kc == 0), stop=(kc == NKC - 1)),
                        waits=(wt15 if kc == 0 else []),
                        inc=("tns" if kc == NKC - 1 else None))
                w1_toks[ht % 2] = b.tok("tns")
                b.op("scalar", lambda e, ht=ht, bk=bk, e_i=e_i: e.activation(
                    arp(o_hdn + ht * 512, 512), pb[bk][:, :], ACTF.Silu,
                    bias=arp(o_rb1 + e_i * 4 + ht, 1), scale=1.0),
                    waits=[w1_toks[ht % 2], hdn_free if ht == 0 else None], inc="act")
                w1_ev[ht % 2] = b.tok("act")
            thdn = b.tok("act")
            if e_i < E:
                for ht in range(4):
                    wt16 = [thdn, cfb_ev] if ht == 0 else []
                    b.op("vector", lambda e, ht=ht: e.tensor_tensor(
                        arp(o_hdn + ht * 512, 512), arp(o_hdn + ht * 512, 512),
                        arp(o_cfb, 512), ALU.mult), waits=wt16,
                        inc=("dve" if ht == 3 else None))
                thdns = b.tok("dve")
                cfb_cons = thdns
            else:
                thdns = thdn
            for ot in range(NKC):
                wslot2 = o_w2r + (ot % 2) * 512
                wtd2 = [w2_toks[ot % 2]] if (e_i > 0 or ot >= 2) else [t_inproj_mm]
                if e_i < E:
                    srcw2 = bass.AP(rw2, e_i * HD * DM + ot * 128, [[DM, 128], [128 * DM, 4], [1, 128]])
                else:
                    srcw2 = bass.AP(sw2, ot * 128, [[DM, 128], [128 * DM, 4], [1, 128]])
                weng = "sync" if ot % 2 == 0 else "gpsimd"
                tok_w2 = b.op(weng, lambda e, wo=wslot2, src=srcw2: e.dma_start(
                    out=bass.AP(ar, wo, [[AF, 128], [128, 4], [1, 128]]), in_=src),
                    waits=wtd2, inc=("dw" if ot % 2 == 0 else "dg"), inck=16)
                bk = 2 + (ot % 2)
                wt18 = [tok_w2, thdns, acc_tok[ot] if e_i > 0 else acc_free_tok]
                if ot >= 2:
                    wt18.append(acc_tok[ot - 2])
                for hc in range(4):
                    b.op("tensor", lambda e, wo=wslot2, hc=hc, bk=bk: e.matmul(
                        pb[bk][:, :], arp(wo + hc * 128, 128), arp(o_hdn + hc * 512, 512),
                        start=(hc == 0), stop=(hc == 3)),
                        waits=(wt18 if hc == 0 else []),
                        inc=("tns" if hc == 3 else None))
                w2_toks[ot % 2] = b.tok("tns")
                tmm2 = w2_toks[ot % 2]
                if e_i == 0:
                    b.op("vector", lambda e, ot=ot, bk=bk: e.tensor_copy(
                        arp(acc_off(ot), 512), pb[bk][:, :]), waits=[tmm2], inc="dve")
                else:
                    b.op("vector", lambda e, ot=ot, bk=bk: e.tensor_tensor(
                        arp(acc_off(ot), 512), arp(acc_off(ot), 512),
                        pb[bk][:, :], ALU.add), waits=[tmm2], inc="dve")
                acc_tok[ot] = b.tok("dve")
            hdn_free = w2_toks[1]
            marks.append(len(b.capm))
        thdn_dead = b.tok("tns")

        bchook(4)
        rb2_toks = [None, None]
        xm_toks = [None, None]
        for ot in range(NKC):
            so = o_rb2s
            dwt = [rb2_toks[(ot + 1) % 2]] if ot >= 1 else [fst.get("rb2s_free")]
            tok_b2 = b.op("sync", lambda e, ot=ot, so=so: e.dma_start(
                out=arp(so, 128, 0, 8),
                in_=bass.AP(rb2, ot * 128, [[DM, 8], [1, 128], [1, 1]])),
                waits=dwt, inc="dw", inck=16)
            bk = 2 + (ot % 2)
            wt19 = [tok_b2, tcf, acc_tok[ot]]
            wt19.append(acc_tok[6 + ot] if ot < 2 else rb2_ev[ot % 2])
            b.op("tensor", lambda e, bk=bk, so=so: e.matmul(
                pb[bk][:, :], arp(so, 128, 0, 8), arp(o_cf, 512, 0, 8),
                start=True, stop=True), waits=wt19, inc="tns")
            tb2 = b.tok("tns")
            rb2_toks[ot % 2] = tb2
            xslot = o_xmres + (ot % 2) * 512
            tok_xm = b.op("gpsimd", lambda e, fc=fc, ot=ot, xs=xslot: e.dma_start(
                out=arp(xs, 512),
                in_=bass.AP(x_mid, ot * 128 * L + fc * TC, [[L, 128], [1, TC], [1, 1]])),
                waits=([thdn_dead, t_xmst] if ot < 2 else [xm_toks[ot % 2]]), inc="dg", inck=16)
            b.op("vector", lambda e, ot=ot, bk=bk: e.tensor_tensor(
                arp(acc_off(ot), 512), arp(acc_off(ot), 512), pb[bk][:, :], ALU.add),
                waits=[tb2], inc="dve")
            rb2_ev[ot % 2] = b.tok("dve")
            b.op("vector", lambda e, ot=ot: e.tensor_scalar(
                arp(acc_off(ot), 512), arp(acc_off(ot), 512),
                arp(o_sb2 + ot, 1), None, ALU.add))
            b.op("vector", lambda e, ot=ot, xs=xslot: e.scalar_tensor_tensor(
                arp(acc_off(ot), 512), arp(acc_off(ot), 512), modcol("gb", ot),
                arp(xs, 512), ALU.mult, ALU.add), waits=[tok_xm], inc="dve")
            xm_toks[ot % 2] = b.tok("dve")
        tfin = b.tok("dve")

        ev_prev = None
        stok = None
        for tt in range(4):
            bk = 6 if tt % 2 == 0 else 7
            wt20 = [tfin, thdn_dead] if tt == 0 else [ev_prev]
            for dc in range(4):
                b.op("tensor", lambda e, tt=tt, dc=dc, bk=bk: e.transpose(
                    pb[bk][:, dc * 128:(dc + 1) * 128],
                    arp(acc_off(dc) + tt * 128, 128),
                    arp(o_eye, 128)),
                    waits=(wt20 if dc == 0 else []),
                    inc=("tns" if dc == 3 else None))
            tt1 = b.tok("tns")
            b.op("scalar", lambda e, bk=bk: e.copy(
                arp(o_outst, 512), pb[bk][:, :]), waits=[tt1], inc="act")
            tev1 = b.tok("act")
            for dc in range(4, 8):
                b.op("tensor", lambda e, tt=tt, dc=dc, bk=bk: e.transpose(
                    pb[bk][:, (dc - 4) * 128:(dc - 3) * 128],
                    arp(acc_off(dc) + tt * 128, 128),
                    arp(o_eye, 128)),
                    waits=([tev1] if dc == 4 else []),
                    inc=("tns" if dc == 7 else None))
            tt2 = b.tok("tns")
            b.op("scalar", lambda e, bk=bk: e.copy(
                arp(o_outst + 512, 512), pb[bk][:, :]), waits=[tt2], inc="act")
            tev = b.tok("act")
            stok = b.op("scalar", lambda e, fc=fc, tt=tt: e.dma_start(
                out=bass.AP(out, (fc * TC + tt * 128) * DM, [[DM, 128], [1, DM], [1, 1]]),
                in_=bass.AP(ar, o_outst, [[AF, 128], [1, 1024], [1, 1]])),
                waits=[tev], inc="st", inck=16)
            ev_prev = stok
        caps = b.end_capture()
        fst["rb2s_free"] = rb2_toks[0]
        return dict(caps=caps, caps_front=caps_front, marks=marks, store_last=stok,
                    tensor_done=b.tok("tns"),
                    dve_done=tfin, thdn_dead=thdn_dead, w1_last=w1_toks, w2_last=w2_toks)

    o_xtok = o_big
    o_xi = o_big
    o_y = o_big

    st_total = 16 * 4 * NCH
    tprev_tensor = None
    tprev_store = None
    tprev_gate = None
    tprev_tailsave = None
    last_state_tok = [None, None]
    hc_tok = [None, None]
    prev_f = None
    prev_f_caps = None

    for ch in range(NCH):
        wt = [tmodv] if ch == 0 else [tprev_tensor, tprev_store]
        tok_x = b.op("scalar", lambda e, ch=ch: e.dma_start(
            out=bass.AP(ar, o_xtok, [[AF, 128], [1024, 4], [1, 1024]]),
            in_=bass.AP(x_in, ch * TC * DM, [[DM, 128], [128 * DM, 4], [1, DM]])),
            waits=wt, inc="dx", inck=16)
        for dc in range(NKC):
            bk = dc % 2
            wt2 = [tok_x, t_eye] if dc == 0 else []
            if dc >= 2:
                wt2.append(("act", b.cnt["act"] - 1))
            for tt in range(4):
                b.op("tensor", lambda e, dc=dc, tt=tt, bk=bk: e.transpose(
                    pb[bk][:, tt * 128:(tt + 1) * 128],
                    arp(o_xtok + tt * 1024 + dc * 128, 128),
                    arp(o_eye, 128)),
                    waits=(wt2 if tt == 0 else []),
                    inc=("tns" if tt == 3 else None))
            ttr = b.tok("tns")
            b.op("scalar", lambda e, dc=dc, bk=bk: e.copy(
                arp(o_xT + dc * 512, 512), pb[bk][:, :]), waits=[ttr], inc="act")
        txT = b.tok("act")

        tln = emit_ln(o_xT, o_hT, "ca1", "sa", [txT],
                      o_R1, o_R2, o_rows, o_rows + 512, o_rows + 1024, o_R1, o_R2)

        evac_toks = []
        ip_toks = [None, None]
        for jt in range(NJT):
            woff = o_wring + (jt % 2) * 1024
            if ch == 0 and jt < 2:
                wt3 = [t_mod_mm]
            elif jt < 2:
                wt3 = [tprev_tensor]
            else:
                wt3 = [ip_toks[jt % 2]]
            weng = "sync" if jt % 2 == 0 else "gpsimd"
            tok_w = b.op(weng, lambda e, jt=jt, wo=woff: e.dma_start(
                out=bass.AP(ar, wo, [[AF, 128], [128, 8], [1, 128]]),
                in_=bass.AP(w_inp, jt * 128, [[4 * DM, 128], [128 * 4 * DM, 8], [1, 128]])),
                waits=wt3, inc=("dw" if jt % 2 == 0 else "dg"), inck=16)
            bk = 2 + (jt % 2)
            wt4 = [tok_w, tln]
            if jt >= 2:
                wt4.append(("act", b.cnt["act"] - 1))
            for kc in range(NKC):
                b.op("tensor", lambda e, wo=woff, kc=kc, bk=bk: e.matmul(
                    pb[bk][:, :], arp(wo + kc * 128, 128), arp(o_hT + kc * 512, 512),
                    start=(kc == 0), stop=(kc == NKC - 1)),
                    waits=(wt4 if kc == 0 else []),
                    inc=("tns" if kc == NKC - 1 else None))
            ip_toks[jt % 2] = b.tok("tns")
            tmm = ip_toks[jt % 2]
            if jt < 16:
                wt5 = [tmm]
                if ch > 0 and jt == 0:
                    wt5.append(tprev_tailsave)
                b.op("scalar", lambda e, jt=jt, bk=bk: e.copy(
                    arp(o_xi + jt * 515 + 3, 512), pb[bk][:, :]), waits=wt5, inc="act")
            else:
                wt5 = [tmm]
                if jt == 16 and prev_f is not None:
                    wt5.append(prev_f["store_last"])
                b.op("scalar", lambda e, jt=jt, bk=bk: e.activation(
                    arp(o_sz + (jt - 16) * 512, 512), pb[bk][:, :], ACTF.Silu),
                    waits=wt5, inc="act")
            evac_toks.append(b.tok("act"))
        t_ip_mm = b.tok("tns")
        tsz_done = b.tok("act")
        t_spill = b.op("gpsimd", lambda e, ch=ch: e.dma_start(
            out=bass.AP(sz_sp, 0, [[TC, 128], [128 * TC, 16], [1, TC]]),
            in_=bass.AP(ar, o_sz, [[AF, 128], [512, 16], [1, 512]])),
            waits=[tsz_done, tprev_gate], inc="dg", inck=16)

        for cc in range(NCC):
            base = o_xi + cc * 515
            wtc = [evac_toks[cc]]
            if cc == 0 and prev_f is not None:
                wtc.append(prev_f["dve_done"])
            b.op("vector", lambda e, cc=cc, base=base: e.tensor_copy(
                arp(base, 3), arp(o_tails + cc * 3, 3)), waits=wtc)
            b.op("vector", lambda e, cc=cc, base=base: e.tensor_scalar(
                arp(o_xc + cc * 512, 512), arp(base, 512), arp(o_convw + cc * 4, 1), None, ALU.mult))
            for tau in range(1, 4):
                b.op("vector", lambda e, cc=cc, base=base, tau=tau: e.scalar_tensor_tensor(
                    arp(o_xc + cc * 512, 512), arp(base + tau, 512), arp(o_convw + cc * 4 + tau, 1),
                    arp(o_xc + cc * 512, 512), ALU.mult, ALU.add),
                    inc=("dve" if tau == 3 else None))
            tcv = b.tok("dve")
            b.op("scalar", lambda e, cc=cc: e.activation(
                arp(o_xc + cc * 512, 512), arp(o_xc + cc * 512, 512), ACTF.Silu,
                bias=arp(o_convb + cc, 1), scale=1.0), waits=[tcv], inc="act")
        tconv = b.tok("act")
        for cc in range(NCC):
            b.op("scalar", lambda e, cc=cc: e.copy(
                arp(o_tails + cc * 3, 3), arp(o_xi + cc * 515 + 512, 3)), inc="act")
        tprev_tailsave = b.tok("act")

        for cc in range(NCC):
            wty = [tconv, tprev_tailsave] if cc == 0 else []
            b.op("vector", lambda e, cc=cc: e.tensor_scalar(
                arp(o_y + cc * 512, 512), arp(o_xc + cc * 512, 512),
                arp(o_dvec + cc, 1), None, ALU.mult), waits=wty,
                inc=("dve" if cc == NCC - 1 else None))
        tyinit = b.tok("dve")

        for cc in range(NCC):
            wt7 = [tconv, t_xp] if cc == 0 else []
            b.op("tensor", lambda e, cc=cc: e.matmul(
                pb[4][0:33, :], arp(o_xpw + cc * 33, 33), arp(o_xc + cc * 512, 512),
                start=(cc == 0), stop=(cc == NCC - 1)),
                waits=wt7, inc=("tns" if cc == NCC - 1 else None))
        txpj = b.tok("tns")
        b.op("scalar", lambda e: e.copy(arp(o_bcdt, 512, 0, 33), pb[4][0:33, :]),
             waits=[txpj], inc="act")
        tbcdt = b.tok("act")
        b.op("tensor", lambda e: e.matmul(pb[5][:, :], arp(o_ones, 128, 0, 1),
                                          arp(o_bcdt, 512, 0, 1), start=True, stop=True),
             waits=[tbcdt], inc="tns")
        tdrawmm = b.tok("tns")
        b.op("scalar", lambda e: e.copy(arp(o_scr2, 512), pb[5][:, :]), waits=[tdrawmm], inc="act")
        tdraw = b.tok("act")

        for g2 in range(16):
            dst = o_hT + (g2 % 8) * 512 if g2 < 8 else o_mdt + (g2 - 8) * 512
            wt8 = [tdraw, t_ip_mm] if g2 == 0 else []
            b.op("scalar", lambda e, g2=g2, dst=dst: e.activation(
                arp(dst, 512), arp(o_scr2, 512), ACTF.Sigmoid,
                bias=arp(o_ndtb + g2, 1), scale=arp(o_ndtw + g2, 1)), waits=wt8)
        for g2 in range(16):
            dst = o_hT + (g2 % 8) * 512 if g2 < 8 else o_mdt + (g2 - 8) * 512
            b.op("scalar", lambda e, g2=g2, dst=dst: e.activation(
                arp(dst, 512), arp(dst, 512), ACTF.Ln), inc="act")
        tmdt = b.tok("act")
        for g2 in range(16):
            src = o_hT + (g2 % 8) * 512 if g2 < 8 else o_mdt + (g2 - 8) * 512
            wt9 = [tmdt, tyinit] if g2 == 0 else []
            b.op("vector", lambda e, g2=g2, src=src: e.tensor_tensor(
                arp(o_dtxm + g2 * 512, 512), arp(src, 512),
                arp(o_dtxm + g2 * 512, 512), ALU.mult), waits=wt9,
                inc=("dve" if g2 == 15 else None))
        tdtx = b.tok("dve")

        bc_emitted = [0]
        bc_mm_toks = [None] * 32
        rowdma_toks = [None] * 32
        prev_row_mm = [None, None]
        bc_base = 32 * ch

        def emit_pair(k):
            s = k % 16
            slB = o_rows if k % 2 == 0 else o_rows + 512
            slC = (o_rows + 1024) if k % 2 == 0 else o_R1
            dwt = [tbcdt, prev_row_mm[k % 2]]
            tokB = b.op("gpsimd", lambda e, s=s, sl=slB: e.dma_start(
                out=arp(sl, 512, 0, 1), in_=ar[1 + s:2 + s, o_bcdt:o_bcdt + 512]),
                waits=dwt, inc="dg", inck=16)
            tokC = b.op("gpsimd", lambda e, s=s, sl=slC: e.dma_start(
                out=arp(sl, 512, 0, 1), in_=ar[17 + s:18 + s, o_bcdt:o_bcdt + 512]),
                inc="dg", inck=16)
            rowdma_toks[k] = tokC
            bkB = 4 if k % 2 == 0 else 0
            bkC = 5 if k % 2 == 0 else 1
            b.op("tensor", lambda e, sl=slB, bkB=bkB, k=k: e.matmul(
                pb[bkB][:, :], arp(o_ones, 128, 0, 1),
                arp(sl, 512, 0, 1), start=True, stop=True),
                waits=[tokC, ("bc", bc_base + max(0, k - 1))] + ([tdraw] if k == 0 else []))
            b.op("tensor", lambda e, sl=slC, bkC=bkC: e.matmul(
                pb[bkC][:, :], arp(o_ones, 128, 0, 1),
                arp(sl, 512, 0, 1), start=True, stop=True), inc="tns")
            bc_mm_toks[k] = b.tok("tns")
            prev_row_mm[k % 2] = bc_mm_toks[k]
            bc_emitted[0] += 1

        def bchook(n=4):
            for _ in range(n):
                if bc_emitted[0] < 32:
                    emit_pair(bc_emitted[0])

        if ch > 0:
            acc_free = prev_f["tensor_done"] if prev_f is not None else None
            f_res = emit_F(ch - 1, t_spill, tprev_store, t_ip_mm, acc_free, bchook)
        else:
            f_res = None

        if f_res is not None:
            b.drain(f_res["caps_front"])
        fcaps = f_res["caps"] if f_res is not None else []
        dtgt = [0] * 32
        if f_res is not None:
            mk = f_res["marks"]
            prev = 0
            for j in range(E + 1):
                cnt = mk[j] - prev
                prev = mk[j]
                base_i = 3 * j + 4 if j < E else 28
                q, r = divmod(cnt, 3)
                for ii in range(3):
                    dtgt[base_i + ii] += q + (1 if ii < r else 0)
            rest = len(fcaps) - mk[E]
            q, r = divmod(rest, 3)
            for ii in range(3):
                dtgt[29 + ii] += q + (1 if ii < r else 0)
        bc_cons = [None, None]
        for dh in range(2):
            for s in range(S):
                k = dh * 16 + s
                while bc_emitted[0] <= k:
                    emit_pair(bc_emitted[0])
                sl = k % 2
                bkB = 4 if sl == 0 else 0
                bkC = 5 if sl == 0 else 1
                b.op("scalar", lambda e, sl=sl, bkB=bkB: e.copy(
                    arp(o_Bbc + sl * 512, 512), pb[bkB][:, :]),
                    waits=[bc_mm_toks[k], bc_cons[sl]])
                b.op("scalar", lambda e, sl=sl, bkC=bkC: e.copy(
                    arp(o_Cbc + sl * 512, 512), pb[bkC][:, :]), inc="act")
                b.op("scalar", lambda e: e.copy(arp(o_lgs + 50, 1), arp(o_lgs + 50, 1)),
                     inc="bc")
                tBC = b.tok("act")
                for g in range(8):
                    ga_ = dh * 8 + g
                    slot = g % 2
                    msrc = (o_hT if dh == 0 else o_mdt) + g * 512
                    wtA = [hc_tok[slot], last_state_tok[slot]]
                    b.op("scalar", lambda e, msrc=msrc, ga_=ga_, s=s, slot=slot: e.activation(
                        arp(o_dAr + slot * 512, 512), arp(msrc, 512), ACTF.Exp,
                        scale=arp(o_nega + ga_ * 16 + s, 1)), waits=wtA, inc="act")
                    tda = b.tok("act")
                    wtB = [tda] + ([tBC, tdtx] if g == 0 else [])
                    b.op("vector", lambda e, ga_=ga_, slot=slot, sl=sl: e.tensor_tensor(
                        arp(o_dBr + slot * 512, 512), arp(o_dtxm + ga_ * 512, 512),
                        arp(o_Bbc + sl * 512, 512), ALU.mult), waits=wtB, inc="dve")
                    b.op("vector", lambda e, ga_=ga_, s=s, slot=slot: e.tensor_tensor_scan(
                        arp(o_hr + slot * 512, 512), arp(o_dAr + slot * 512, 512),
                        arp(o_dBr + slot * 512, 512), arp(o_state + ga_ * 16 + s, 1),
                        ALU.mult, ALU.subtract), inc="dve")
                    tscan = b.tok("dve")
                    b.op("scalar", lambda e, ga_=ga_, s=s, slot=slot: e.copy(
                        arp(o_state + ga_ * 16 + s, 1), arp(o_hr + slot * 512 + 511, 1)),
                        waits=[tscan], inc="act")
                    last_state_tok[slot] = b.tok("act")
                    b.op("vector", lambda e, slot=slot, sl=sl: e.tensor_tensor(
                        arp(o_dBr + slot * 512, 512), arp(o_hr + slot * 512, 512),
                        arp(o_Cbc + sl * 512, 512), ALU.mult), inc="dve")
                    hc_tok[slot] = b.tok("dve")
                    b.op("vector", lambda e, ga_=ga_, slot=slot: e.tensor_tensor(
                        arp(o_y + ga_ * 512, 512), arp(o_y + ga_ * 512, 512),
                        arp(o_dBr + slot * 512, 512), ALU.add), inc="dve")
                bc_cons[sl] = b.tok("dve")
                b.drain(fcaps, dtgt[k])
        b.drain(fcaps)
        tscan_all = b.tok("dve")

        gate_dma = [None, None]
        for cc in range(NCC):
            gsl = o_rows if cc % 2 == 0 else o_rows + 512
            gwt = [tscan_all if cc < 2 else None, t_spill if cc < 2 else None,
                   gate_dma[cc % 2]]
            gtok = b.op("gpsimd", lambda e, cc=cc, gs=gsl: e.dma_start(
                out=arp(gs, 512),
                in_=bass.AP(sz_sp, cc * 128 * TC, [[TC, 128], [1, TC], [1, 1]])),
                waits=gwt, inc="dg", inck=16)
            wtg = [gtok] + ([tscan_all] if cc == 0 else [])
            b.op("vector", lambda e, cc=cc, gs=gsl: e.tensor_tensor(
                arp(o_y + cc * 512, 512), arp(o_y + cc * 512, 512),
                arp(gs, 512), ALU.mult), waits=wtg, inc="dve")
            gate_dma[cc % 2] = b.tok("dve")
        tgate = b.tok("dve")
        tprev_gate = tgate

        op_toks = [None]
        for ot in range(NKC):
            woff = o_wring
            if ot == 0:
                wt10 = [f_res["thdn_dead"] if f_res is not None else t_ip_mm]
            else:
                wt10 = [op_toks[0]]
            tok_w = b.op("sync", lambda e, ot=ot, wo=woff: e.dma_start(
                out=bass.AP(ar, wo, [[AF, 128], [128, 16], [1, 128]]),
                in_=bass.AP(w_outp, ot * 128, [[DM, 128], [128 * DM, 16], [1, 128]])),
                waits=wt10, inc="dw", inck=16)
            bk = 2 + (ot % 2)
            wt11 = [tok_w, tgate]
            if ot >= 2:
                wt11.append(("dve", b.cnt["dve"] - 1))
            for cc in range(NCC):
                b.op("tensor", lambda e, wo=woff, cc=cc, bk=bk: e.matmul(
                    pb[bk][:, :], arp(wo + cc * 128, 128), arp(o_y + cc * 512, 512),
                    start=(cc == 0), stop=(cc == NCC - 1)),
                    waits=(wt11 if cc == 0 else []),
                    inc=("tns" if cc == NCC - 1 else None))
            op_toks[0] = b.tok("tns")
            tmo = op_toks[0]
            b.op("vector", lambda e, ot=ot, bk=bk: e.scalar_tensor_tensor(
                arp(o_xT + ot * 512, 512), pb[bk][:, :], modcol("ga", ot),
                arp(o_xT + ot * 512, 512), ALU.mult, ALU.add), waits=[tmo], inc="dve")
        tresid = b.tok("dve")
        tprev_tensor = b.tok("tns")
        tprev_store = b.op("gpsimd", lambda e, ch=ch: e.dma_start(
            out=bass.AP(x_mid, ch * TC, [[L, 128], [128 * L, 8], [1, TC]]),
            in_=bass.AP(ar, o_xT, [[AF, 128], [512, 8], [1, 512]])),
            waits=[tresid], inc="dg", inck=16)
        prev_f = f_res

    f_res = emit_F(NCH - 1,
                   prev_f["store_last"] if prev_f is not None else tprev_store,
                   tprev_store,
                   tprev_tensor,
                   prev_f["tensor_done"] if prev_f is not None else None,
                   lambda n=4: None)
    b.drain(f_res["caps_front"])
    b.drain(f_res["caps"])

    b.waits("gpsimd", [("st", st_total)])

    with nc.Block() as block:
        @block.sync
        def _(e):
            for f in b.qs["sync"]:
                f(e)

        @block.scalar
        def _(e):
            for f in b.qs["scalar"]:
                f(e)

        @block.vector
        def _(e):
            for f in b.qs["vector"]:
                f(e)

        @block.tensor
        def _(e):
            for f in b.qs["tensor"]:
                f(e)

        @block.gpsimd
        def _(e):
            for f in b.qs["gpsimd"]:
                f(e)

    stack.close()
    return nc


_NC_CACHE = None


def kernel(**inputs):
    global _NC_CACHE
    if _NC_CACHE is None:
        _NC_CACHE = build_program()
    nc = _NC_CACHE
    x = np.ascontiguousarray(inputs["L0_attn_x"], dtype=np.float32)
    base = {
        "in_proj_w": np.ascontiguousarray(inputs["L0_attn_in_proj_w"], np.float32),
        "conv_w": np.ascontiguousarray(inputs["L0_attn_conv_w"], np.float32),
        "conv_b": np.ascontiguousarray(inputs["L0_attn_conv_b"], np.float32),
        "x_proj_w": np.ascontiguousarray(inputs["L0_attn_x_proj_w"], np.float32),
        "dt_w": np.ascontiguousarray(inputs["L0_attn_dt_w"], np.float32),
        "dt_b": np.ascontiguousarray(inputs["L0_attn_dt_b"], np.float32),
        "A_log": np.ascontiguousarray(inputs["L0_attn_A_log"], np.float32),
        "D": np.ascontiguousarray(inputs["L0_attn_D"], np.float32),
        "out_proj_w": np.ascontiguousarray(inputs["L0_attn_out_proj_w"], np.float32),
        "routing_W": np.ascontiguousarray(inputs["L0_ffn_routing_W"], np.float32),
        "expert_bias": np.ascontiguousarray(inputs["L0_ffn_expert_bias"], np.float32),
        "sW1": np.ascontiguousarray(inputs["L0_ffn_sW1"], np.float32),
        "sb1": np.ascontiguousarray(inputs["L0_ffn_sb1"], np.float32),
        "sW2": np.ascontiguousarray(inputs["L0_ffn_sW2"], np.float32),
        "sb2": np.ascontiguousarray(inputs["L0_ffn_sb2"], np.float32),
        "rW1": np.ascontiguousarray(inputs["L0_ffn_rW1"], np.float32),
        "rb1": np.ascontiguousarray(inputs["L0_ffn_rb1"], np.float32),
        "rW2": np.ascontiguousarray(inputs["L0_ffn_rW2"], np.float32),
        "rb2": np.ascontiguousarray(inputs["L0_ffn_rb2"], np.float32),
        "wmod": np.ascontiguousarray(inputs["L0_wmod"], np.float32),
        "eye": np.eye(128, dtype=np.float32),
        "onesrow": np.ones((1, 128), np.float32),
    }
    cs = np.ascontiguousarray(inputs["L0_c"], np.float32)
    in_maps = []
    for b_i in range(8):
        m = dict(base)
        m["x"] = x[b_i]
        m["c_vec"] = cs[b_i]
        in_maps.append(m)
    res = run_bass_kernel_spmd(nc, in_maps, core_ids=list(range(8)))
    return np.stack([r["out"] for r in res.results], axis=0)


if __name__ == "__main__":
    build_program()
    print("built ok")


# version 08
# speedup vs baseline: 1.0410x; 1.0410x over previous
import numpy as np
import contextlib
import concourse.bass as bass
import concourse.mybir as mybir
from concourse.bass_utils import run_bass_kernel_spmd

F32 = mybir.dt.float32
ALU = mybir.AluOpType
ACTF = mybir.ActivationFunctionType
AXX = mybir.AxisListType.X

L, DM, DI, S, E, HD = 4096, 1024, 2048, 16, 8, 512
TC = 512
NCH = L // TC
NKC = DM // 128
NCC = DI // 128
NJT = 4 * DM // 128
EPS = 1e-5

ENGS = ("sync", "scalar", "vector", "tensor", "gpsimd")
DEBUG = False


class Bld:
    def __init__(self, nc):
        self.nc = nc
        self.qs = {e: [] for e in ENGS}
        self.sh = {}
        self.cnt = {}
        self.cap = None
        self.alias = {}

    def begin_capture(self, engines, alias=None):
        self.cap = set(engines)
        self.capm = []
        self.alias = alias or {}

    def end_capture(self):
        c = self.capm
        self.cap = None
        self.capm = None
        self.alias = {}
        return c

    def drain(self, caplist, n=None):
        k = len(caplist) if n is None else min(n, len(caplist))
        for (eng, f) in caplist[:k]:
            self.qs[eng].append(f)
        del caplist[:k]

    def new_sem(self, name, stack):
        h = stack.enter_context(self.nc.semaphore(name))
        self.sh[name] = h
        self.cnt[name] = 0

    def tok(self, name):
        name = self.alias.get(name, name)
        return (name, self.cnt[name])

    def op(self, eng, build, waits=(), inc=None, inck=1):
        waits = [t for t in waits if t is not None]
        if inc is not None:
            inc = self.alias.get(inc, inc)
            self.cnt[inc] += inck
        sh = self.sh
        incv = inc

        def f(e):
            for (s, v) in waits:
                e.wait_ge(sh[s], v)
            ins = build(e)
            if incv is not None:
                ins.then_inc(sh[incv], inck)
        if self.cap is not None and eng in self.cap:
            self.capm.append((eng, f))
        else:
            self.qs[eng].append(f)
        return self.tok(inc) if inc is not None else None

    def waits(self, eng, waits):
        waits = [t for t in waits if t is not None]
        sh = self.sh

        def f(e):
            for (s, v) in waits:
                e.wait_ge(sh[s], v)
        self.qs[eng].append(f)


class Arena:
    def __init__(self, total):
        self.total = total
        self.off = 0

    def alloc(self, n, name=""):
        assert self.off + n <= self.total, f"arena overflow at {name}: {self.off}+{n}"
        o = self.off
        self.off += n
        return o


def build_program():
    nc = bass.Bass("TRN2", target_bir_lowering=False, detect_race_conditions=False)
    DT = F32

    x_in = nc.dram_tensor("x", [L, DM], DT, kind="ExternalInput")
    w_inp = nc.dram_tensor("in_proj_w", [DM, 4 * DM], DT, kind="ExternalInput")
    w_conv = nc.dram_tensor("conv_w", [DI, 1, 4], DT, kind="ExternalInput")
    b_conv = nc.dram_tensor("conv_b", [DI], DT, kind="ExternalInput")
    w_xp = nc.dram_tensor("x_proj_w", [DI, 33], DT, kind="ExternalInput")
    dt_w = nc.dram_tensor("dt_w", [DI], DT, kind="ExternalInput")
    dt_b = nc.dram_tensor("dt_b", [DI], DT, kind="ExternalInput")
    a_log = nc.dram_tensor("A_log", [DI, S], DT, kind="ExternalInput")
    d_vec = nc.dram_tensor("D", [DI], DT, kind="ExternalInput")
    w_outp = nc.dram_tensor("out_proj_w", [DI, DM], DT, kind="ExternalInput")
    w_rt = nc.dram_tensor("routing_W", [E, DM], DT, kind="ExternalInput")
    b_e = nc.dram_tensor("expert_bias", [E], DT, kind="ExternalInput")
    sw1 = nc.dram_tensor("sW1", [DM, HD], DT, kind="ExternalInput")
    sb1 = nc.dram_tensor("sb1", [HD], DT, kind="ExternalInput")
    sw2 = nc.dram_tensor("sW2", [HD, DM], DT, kind="ExternalInput")
    sb2 = nc.dram_tensor("sb2", [DM], DT, kind="ExternalInput")
    rw1 = nc.dram_tensor("rW1", [E, DM, HD], DT, kind="ExternalInput")
    rb1 = nc.dram_tensor("rb1", [E, HD], DT, kind="ExternalInput")
    rw2 = nc.dram_tensor("rW2", [E, HD, DM], DT, kind="ExternalInput")
    rb2 = nc.dram_tensor("rb2", [E, DM], DT, kind="ExternalInput")
    c_in = nc.dram_tensor("c_vec", [DM], DT, kind="ExternalInput")
    w_mod = nc.dram_tensor("wmod", [DM, 6 * DM], DT, kind="ExternalInput")
    eye_in = nc.dram_tensor("eye", [128, 128], DT, kind="ExternalInput")
    ones_in = nc.dram_tensor("onesrow", [1, 128], DT, kind="ExternalInput")

    x_mid = nc.dram_tensor("x_mid", [DM, L], DT)
    sz_sp = nc.dram_tensor("sz_sp", [DI, TC], DT)
    out = nc.dram_tensor("out", [L, DM], DT, kind="ExternalOutput")

    AF = 53200
    stack = contextlib.ExitStack()
    ar = stack.enter_context(nc.sbuf_tensor("arena", [128, AF], DT))
    pb = [stack.enter_context(nc.psum_tensor(f"pb{i}", [128, 512], F32)) for i in range(8)]

    b = Bld(nc)
    for s in ("dw", "dx", "dg", "tns", "act", "dve", "st", "bc", "fdv", "fac"):
        b.new_sem(s, stack)

    A = Arena(AF)
    o_eye = A.alloc(128, "eye")
    o_ones = A.alloc(128, "ones_row")
    o_onescol = A.alloc(1, "ones_col")
    o_mod = A.alloc(64, "mod")
    o_ndtw = A.alloc(16, "ndtw")
    o_ndtb = A.alloc(16, "ndtb")
    o_nega = A.alloc(256, "negA")
    o_convw = A.alloc(64, "convw")
    o_convb = A.alloc(16, "convb")
    o_dvec = A.alloc(16, "Dvec")
    o_xpw = A.alloc(528, "xpw")
    o_state = A.alloc(256, "state")
    o_rt = A.alloc(64, "rtw")
    o_eb = A.alloc(1, "eb")
    o_rb1 = A.alloc(36, "rb1")
    o_sb2 = A.alloc(8, "sb2cols")
    o_tails = A.alloc(48, "tails")
    o_rows = A.alloc(3 * 512, "rows")
    o_bcdt = A.alloc(512, "bcdt")
    o_scr2 = A.alloc(512, "scr2")
    o_R1 = A.alloc(512, "R1")
    o_R2 = A.alloc(512, "R2")
    o_dAr = A.alloc(2 * 512, "dA_ring")
    o_dBr = A.alloc(2 * 512, "dB_ring")
    o_Bbc = A.alloc(2 * 512, "Bbc")
    o_Cbc = A.alloc(2 * 512, "Cbc")
    o_lgs = A.alloc(64, "lgscratch")
    o_rb2s = A.alloc(128, "rb2stat")
    o_fex = A.alloc(5 * 512, "acc_extra")
    o_xT = A.alloc(8 * 512, "xT")
    o_hT = A.alloc(8 * 512, "hT")
    o_big = A.alloc(16 * 515, "big")
    o_xc = A.alloc(16 * 512, "xc")
    o_sz = A.alloc(16 * 512, "sz")
    o_mdt = A.alloc(8 * 512, "mdt")
    o_wring = A.alloc(2 * 2048, "wring")
    assert A.off <= AF, A.off

    o_dtxm = o_xc
    o_Xh = o_sz
    o_hdn = o_sz + 4096
    o_cfb = o_sz + 6144
    o_accB = o_sz + 6656
    o_outst = o_sz
    o_xmres = o_sz + 1024
    o_fsqA = o_sz + 4096
    o_fsqB = o_sz + 4608
    o_fr0 = o_sz + 5120
    o_fr1 = o_sz + 5632
    o_fr2 = o_sz + 6144
    o_fR1 = o_sz + 6656
    o_fR2 = o_sz + 7168
    o_w1r = o_wring
    o_w2r = o_wring + 2048
    o_lg = o_scr2
    o_cf = o_R2
    o_hr = o_wring + 3072

    def acc_off(ot):
        return (o_fex + ot * 512) if ot < 5 else (o_accB + (ot - 5) * 512)

    def arp(off, nf, p0=0, np_=128):
        return ar[p0:p0 + np_, off:off + nf]

    def dma(eng, sem, dst, src, waits=(), slow=False):
        return b.op(eng, lambda e, d=dst, s=src, sl=slow: e.dma_start(
            out=d, in_=s, allow_slow_non_contiguous=sl),
            waits=waits, inc=sem, inck=16)

    t_eye = dma("sync", "dw", arp(o_eye, 128), eye_in.ap())
    dma("sync", "dw", arp(o_ones, 128, 0, 1), ones_in.ap())

    def colmajor_src(t, n):
        return bass.AP(t, 0, [[1, 128], [128, n], [1, 1]])

    def colmajor_dst(off, n):
        return bass.AP(ar, off, [[AF, 128], [1, n], [1, 1]])

    dma("sync", "dw", colmajor_dst(o_ndtw, 16), colmajor_src(dt_w, 16), slow=True)
    dma("sync", "dw", colmajor_dst(o_ndtb, 16), colmajor_src(dt_b, 16), slow=True)
    dma("sync", "dw", colmajor_dst(o_convb, 16), colmajor_src(b_conv, 16), slow=True)
    dma("sync", "dw", colmajor_dst(o_dvec, 16), colmajor_src(d_vec, 16), slow=True)
    dma("sync", "dw", bass.AP(ar, o_nega, [[AF, 128], [16, 16], [1, 16]]),
        bass.AP(a_log, 0, [[16, 128], [128 * 16, 16], [1, 16]]))
    dma("sync", "dw", bass.AP(ar, o_convw, [[AF, 128], [4, 16], [1, 4]]),
        bass.AP(w_conv, 0, [[4, 128], [128 * 4, 16], [1, 4]]))
    t_xp = dma("sync", "dw", bass.AP(ar, o_xpw, [[AF, 128], [33, 16], [1, 33]]),
               bass.AP(w_xp, 0, [[33, 128], [128 * 33, 16], [1, 33]]))
    for kc in range(NKC):
        t_rt = dma("sync", "dw", bass.AP(ar, o_rt + kc * 8, [[AF, 128], [1, 8]]),
                   bass.AP(w_rt, kc * 128, [[1, 128], [DM, 8]]), slow=True)
    t_ebi = dma("sync", "dw", arp(o_eb, 1, 0, 8), bass.AP(b_e, 0, [[1, 8], [1, 1]]), slow=True)
    dma("sync", "dw", bass.AP(ar, o_rb1, [[AF, 128], [4, 8], [1, 4]]),
        bass.AP(rb1, 0, [[1, 128], [HD, 8], [128, 4]]), slow=True)
    dma("sync", "dw", bass.AP(ar, o_rb1 + 32, [[AF, 128], [1, 4], [1, 1]]),
        bass.AP(sb1, 0, [[1, 128], [128, 4], [1, 1]]), slow=True)
    dma("sync", "dw", colmajor_dst(o_sb2, 8), colmajor_src(sb2, 8), slow=True)
    dma("sync", "dw", colmajor_dst(o_mod + 56, 8), colmajor_src(c_in, 8), slow=True)
    tw0 = b.tok("dw")

    b.op("vector", lambda e: e.memset(arp(o_onescol, 1), 1.0), waits=[tw0])
    b.op("vector", lambda e: e.tensor_scalar(arp(o_ndtw, 16), arp(o_ndtw, 16), -1.0, None, ALU.mult))
    b.op("vector", lambda e: e.tensor_scalar(arp(o_ndtb, 16), arp(o_ndtb, 16), -1.0, None, ALU.mult))
    b.op("vector", lambda e: e.tensor_scalar(arp(o_nega, 256), arp(o_nega, 256), 6.0, None, ALU.min))
    b.op("vector", lambda e: e.tensor_scalar(arp(o_nega, 256), arp(o_nega, 256), -6.0, None, ALU.max))
    b.op("vector", lambda e: e.memset(arp(o_state, 256), 0.0))
    b.op("vector", lambda e: e.memset(arp(o_tails, 48), 0.0))
    b.op("vector", lambda e: e.memset(arp(o_lgs + 50, 1), 0.0), inc="dve")
    tconst = b.tok("dve")
    b.op("scalar", lambda e: e.activation(arp(o_nega, 256), arp(o_nega, 256), ACTF.Exp),
         waits=[tconst])
    b.op("scalar", lambda e: e.activation(arp(o_mod + 56, 8), arp(o_mod + 56, 8), ACTF.Silu), inc="act")
    tsc = b.tok("act")

    mm_toks = [None, None]
    ev_toks = [None, None]
    for mt in range(48):
        woff = o_wring + (mt % 2) * 2048
        wt = [mm_toks[mt % 2]] if mt >= 2 else []
        tok_w = b.op("sync", lambda e, mo=mt, wo=woff: e.dma_start(
            out=bass.AP(ar, wo, [[AF, 128], [128, 8], [1, 128]]),
            in_=bass.AP(w_mod, mo * 128, [[6 * DM, 128], [128 * 6 * DM, 8], [1, 128]])),
            waits=wt, inc="dw", inck=16)
        awt = [ev_toks[mt % 2]] if mt >= 2 else []
        for kc in range(NKC):
            b.op("tensor", lambda e, wo=woff, kc=kc, mt=mt: e.matmul(
                pb[mt % 2][:, 0:1], arp(wo + kc * 128, 128), arp(o_mod + 56 + kc, 1),
                start=(kc == 0), stop=(kc == NKC - 1)),
                waits=(([tok_w, tsc] + awt) if kc == 0 else []),
                inc=("tns" if kc == NKC - 1 else None))
        mm_toks[mt % 2] = b.tok("tns")
        b.op("scalar", lambda e, mt=mt: e.copy(arp(o_mod + mt, 1), pb[mt % 2][:, 0:1]),
             waits=[mm_toks[mt % 2]], inc="act")
        ev_toks[mt % 2] = b.tok("act")
    t_mod_mm = b.tok("tns")
    tmod = b.tok("act")
    b.op("vector", lambda e: e.tensor_scalar(arp(o_mod + 8, 8), arp(o_mod + 8, 8), 1.0, None, ALU.add),
         waits=[tmod], inc="dve")
    b.op("vector", lambda e: e.tensor_scalar(arp(o_mod + 32, 8), arp(o_mod + 32, 8), 1.0, None, ALU.add), inc="dve")
    for base in (16, 40):
        b.op("scalar", lambda e, ba=base: e.activation(arp(o_mod + 48, 8), arp(o_mod + ba, 8), ACTF.Abs),
             waits=[tmod, b.tok("dve")], inc="act")
        b.op("vector", lambda e: e.tensor_scalar(arp(o_mod + 48, 8), arp(o_mod + 48, 8), 0.25, None, ALU.max),
             waits=[b.tok("act")], inc="dve")
        b.op("vector", lambda e, ba=base: e.tensor_scalar(arp(o_mod + 56, 8), arp(o_mod + ba, 8), 0.0, None, ALU.is_ge),
             waits=[b.tok("dve")], inc="dve")
        b.op("vector", lambda e: e.tensor_scalar(arp(o_mod + 56, 8), arp(o_mod + 56, 8), 2.0, None, ALU.mult),
             waits=[b.tok("dve")], inc="dve")
        b.op("vector", lambda e: e.tensor_scalar(arp(o_mod + 56, 8), arp(o_mod + 56, 8), -1.0, None, ALU.add),
             waits=[b.tok("dve")], inc="dve")
        b.op("vector", lambda e, ba=base: e.tensor_tensor(arp(o_mod + ba, 8), arp(o_mod + 48, 8), arp(o_mod + 56, 8), ALU.mult),
             waits=[b.tok("dve")], inc="dve")
    tmodv = b.tok("dve")

    MOD = dict(sa=0, ca1=8, ga=16, sb=24, cb1=32, gb=40)

    def modcol(name, j):
        return arp(o_mod + MOD[name] + j, 1)

    def emit_ln(src_off, dst_off, scale_name, shift_name, waits_first,
                sqA, sqB, r0, r1, r2, R1o, R2o):
        for kc in range(NKC):
            wt = list(waits_first) if kc == 0 else []
            b.op("tensor", lambda e, kc=kc: e.matmul(
                pb[6][0:1, :], arp(o_onescol, 1), arp(src_off + kc * 512, 512),
                start=(kc == 0), stop=(kc == NKC - 1)),
                waits=wt, inc=("tns" if kc == NKC - 1 else None))
        tsum = b.tok("tns")
        sqtok = [None, None]
        for kc in range(NKC):
            so = sqA if kc % 2 == 0 else sqB
            wt = list(waits_first) if kc == 0 else []
            if kc >= 2:
                wt.append(sqtok[kc % 2])
            b.op("scalar", lambda e, kc=kc, so=so: e.activation(
                arp(so, 512), arp(src_off + kc * 512, 512), ACTF.Square),
                waits=wt, inc="act")
            ta = b.tok("act")
            b.op("tensor", lambda e, kc=kc, so=so: e.matmul(
                pb[7][0:1, :], arp(o_onescol, 1), arp(so, 512),
                start=(kc == 0), stop=(kc == NKC - 1)),
                waits=[ta], inc="tns")
            sqtok[kc % 2] = b.tok("tns")
        tsq = b.tok("tns")
        b.op("vector", lambda e: e.tensor_scalar(arp(r0, 512, 0, 1), pb[6][0:1, :], 1.0 / DM, None, ALU.mult),
             waits=[tsum, tsq])
        b.op("vector", lambda e: e.tensor_scalar(arp(r1, 512, 0, 1), pb[7][0:1, :], 1.0 / DM, None, ALU.mult))
        b.op("vector", lambda e: e.tensor_tensor(arp(r2, 512, 0, 1), arp(r0, 512, 0, 1), arp(r0, 512, 0, 1), ALU.mult))
        b.op("vector", lambda e: e.tensor_tensor(arp(r1, 512, 0, 1), arp(r1, 512, 0, 1), arp(r2, 512, 0, 1), ALU.subtract))
        b.op("vector", lambda e: e.tensor_scalar(arp(r1, 512, 0, 1), arp(r1, 512, 0, 1), EPS, None, ALU.add), inc="dve")
        tv = b.tok("dve")
        b.op("scalar", lambda e: e.activation(arp(r1, 512, 0, 1), arp(r1, 512, 0, 1), ACTF.Ln), waits=[tv])
        b.op("scalar", lambda e: e.activation(arp(r1, 512, 0, 1), arp(r1, 512, 0, 1), ACTF.Exp, scale=-0.5), inc="act")
        ti = b.tok("act")
        b.op("vector", lambda e: e.tensor_tensor(arp(r2, 512, 0, 1), arp(r0, 512, 0, 1), arp(r1, 512, 0, 1), ALU.mult),
             waits=[ti])
        b.op("vector", lambda e: e.tensor_scalar(arp(r2, 512, 0, 1), arp(r2, 512, 0, 1), -1.0, None, ALU.mult), inc="dve")
        trows = b.tok("dve")
        b.op("tensor", lambda e: e.matmul(pb[6][:, :], arp(o_ones, 128, 0, 1),
                                          arp(r1, 512, 0, 1), start=True, stop=True), waits=[trows])
        b.op("tensor", lambda e: e.matmul(pb[7][:, :], arp(o_ones, 128, 0, 1),
                                          arp(r2, 512, 0, 1), start=True, stop=True), inc="tns")
        tbc = b.tok("tns")
        b.op("scalar", lambda e: e.copy(arp(R1o, 512), pb[6][:, :]), waits=[tbc])
        b.op("scalar", lambda e: e.copy(arp(R2o, 512), pb[7][:, :]), inc="act")
        tR = b.tok("act")
        for kc in range(NKC):
            wt = [tR] if kc == 0 else []
            b.op("vector", lambda e, kc=kc: e.tensor_tensor(
                arp(dst_off + kc * 512, 512), arp(src_off + kc * 512, 512), arp(R1o, 512), ALU.mult), waits=wt)
            b.op("vector", lambda e, kc=kc: e.tensor_tensor(
                arp(dst_off + kc * 512, 512), arp(dst_off + kc * 512, 512), arp(R2o, 512), ALU.add))
            b.op("vector", lambda e, kc=kc, sn=scale_name, shn=shift_name: e.tensor_scalar(
                arp(dst_off + kc * 512, 512), arp(dst_off + kc * 512, 512),
                modcol(sn, kc), modcol(shn, kc), ALU.mult, ALU.add), inc="dve")
        return b.tok("dve")

    fst = {"rb2_last": None, "w1_last": None, "w2_last": [None, None],
           "w1dma_last": [None, None], "acc_free": None, "thdn_dead": None,
           "xh_dead": None, "store_last": None}

    def emit_F(fc, t_spill, t_xmst, t_inproj_mm, acc_free_tok, bchook):
        b.begin_capture(["vector", "scalar"], alias={"dve": "fdv", "act": "fac"})
        tok_x = b.op("gpsimd", lambda e, fc=fc: e.dma_start(
            out=bass.AP(ar, o_Xh, [[AF, 128], [512, 8], [1, TC]]),
            in_=bass.AP(x_mid, fc * TC, [[L, 128], [128 * L, 8], [1, TC]])),
            waits=[t_spill, t_xmst], inc="dg", inck=16)
        tln2 = emit_ln(o_Xh, o_Xh, "cb1", "sb", [tok_x, acc_free_tok],
                       o_fsqA, o_fsqB, o_fr0, o_fr1, o_fr2, o_fR1, o_fR2)
        for kc in range(NKC):
            wt12 = [tln2, t_rt, t_ebi] if kc == 0 else []
            b.op("tensor", lambda e, kc=kc: e.matmul(
                pb[6][0:8, :], arp(o_rt + kc * 8, 8), arp(o_Xh + kc * 512, 512),
                start=(kc == 0), stop=(kc == NKC - 1)),
                waits=wt12, inc=("tns" if kc == NKC - 1 else None))
        trt = b.tok("tns")
        b.op("vector", lambda e: e.tensor_scalar(arp(o_lg, 512, 0, 8), pb[6][0:8, :],
                                                 arp(o_eb, 1, 0, 8), None, ALU.add),
             waits=[trt], inc="dve")
        tlg = b.tok("dve")
        for tt in range(4):
            b.op("tensor", lambda e, tt=tt: e.transpose(
                pb[7][:, tt * 8:(tt + 1) * 8],
                arp(o_lg + tt * 128, 128, 0, 8),
                arp(o_eye, 8, 0, 8)),
                waits=([tlg, t_eye] if tt == 0 else []), inc=("tns" if tt == 3 else None))
        tlgT = b.tok("tns")
        b.op("scalar", lambda e: e.copy(arp(o_lgs, 32), pb[7][:, 0:32]), waits=[tlgT], inc="act")
        tlgT2 = b.tok("act")
        for tt in range(4):
            lo = o_lgs + tt * 8
            sA, sB, sm1, sm2 = o_lgs + 32, o_lgs + 40, o_lgs + 48, o_lgs + 49
            wt13 = [tlgT2] if tt == 0 else []
            b.op("vector", lambda e, lo=lo, sm1=sm1: e.tensor_reduce(
                arp(sm1, 1), arp(lo, 8), AXX, ALU.max), waits=wt13, inc="dve")
            b.op("vector", lambda e, lo=lo, sA=sA, sm1=sm1: e.tensor_scalar(
                arp(sA, 8), arp(lo, 8), arp(sm1, 1), None, ALU.is_equal),
                waits=[b.tok("dve")], inc="dve")
            b.op("vector", lambda e, lo=lo, sA=sA: e.scalar_tensor_tensor(
                arp(sA, 8), arp(sA, 8), -1e30, arp(lo, 8), ALU.mult, ALU.add),
                waits=[b.tok("dve")], inc="dve")
            b.op("vector", lambda e, sA=sA, sm2=sm2: e.tensor_reduce(
                arp(sm2, 1), arp(sA, 8), AXX, ALU.max),
                waits=[b.tok("dve")], inc="dve")
            b.op("vector", lambda e, lo=lo, sA=sA, sm2=sm2: e.tensor_scalar(
                arp(sA, 8), arp(lo, 8), arp(sm2, 1), None, ALU.is_ge),
                waits=[b.tok("dve")], inc="dve")
            tge = b.tok("dve")
            b.op("scalar", lambda e, lo=lo, sB=sB: e.activation(
                arp(sB, 8), arp(lo, 8), ACTF.Sigmoid, scale=-1.0), waits=[tge], inc="act")
            b.op("scalar", lambda e, sB=sB: e.activation(arp(sB, 8), arp(sB, 8), ACTF.Ln),
                 waits=[b.tok("act")], inc="act")
            b.op("scalar", lambda e, sB=sB: e.activation(arp(sB, 8), arp(sB, 8), ACTF.Ln, scale=-1.0),
                 waits=[b.tok("act")], inc="act")
            b.op("scalar", lambda e, sB=sB: e.activation(arp(sB, 8), arp(sB, 8), ACTF.Exp, scale=0.5),
                 waits=[b.tok("act")], inc="act")
            tact = b.tok("act")
            b.op("vector", lambda e, lo=lo, sA=sA, sB=sB: e.tensor_tensor(
                arp(lo, 8), arp(sB, 8), arp(sA, 8), ALU.mult), waits=[tact], inc="dve")
        tcfT = b.tok("dve")
        for tt in range(4):
            b.op("tensor", lambda e, tt=tt: e.transpose(
                pb[7][0:8, tt * 128:(tt + 1) * 128],
                arp(o_lgs + tt * 8, 8),
                arp(o_eye, 128)),
                waits=([tcfT] if tt == 0 else []), inc=("tns" if tt == 3 else None))
        tcmb = b.tok("tns")
        b.op("scalar", lambda e: e.copy(arp(o_cf, 512, 0, 8), pb[7][0:8, :]), waits=[tcmb], inc="act")
        tcf = b.tok("act")
        caps_front = b.end_capture()
        b.begin_capture(["vector", "scalar"], alias={"dve": "fdv", "act": "fac"})
        bchook()

        w1_toks = [None, None]
        w2_toks = [None, None]
        w1_ev = [None, None]
        rb2_ev = [None, None]
        hdn_free = None
        cfb_ev = None
        cfb_cons = None
        cfb_mm_prev = None
        acc_tok = [None] * NKC
        marks = []
        for e_i in range(E + 1):
            bchook(3)
            if e_i < E:
                trow = b.op("gpsimd", lambda e, e_i=e_i: e.dma_start(
                    out=arp(o_scr2, 512, 0, 1),
                    in_=ar[e_i:e_i + 1, o_cf:o_cf + 512]),
                    waits=[tcf, cfb_mm_prev], inc="dg", inck=16)
                b.op("tensor", lambda e: e.matmul(
                    pb[6][:, :], arp(o_ones, 128, 0, 1),
                    arp(o_scr2, 512, 0, 1), start=True, stop=True),
                    waits=[trow, cfb_ev], inc="tns")
                tcb = b.tok("tns")
                cfb_mm_prev = tcb
                b.op("scalar", lambda e: e.copy(arp(o_cfb, 512), pb[6][:, :]),
                     waits=[tcb, cfb_cons], inc="act")
                cfb_ev = b.tok("act")
            for ht in range(4):
                wslot = o_w1r + (ht % 2) * 1024
                if e_i == 0 and ht < 2:
                    wtd = [t_inproj_mm]
                else:
                    wtd = [w1_toks[ht % 2]]
                if e_i < E:
                    srcw = bass.AP(rw1, e_i * DM * HD + ht * 128, [[HD, 128], [128 * HD, 8], [1, 128]])
                else:
                    srcw = bass.AP(sw1, ht * 128, [[HD, 128], [128 * HD, 8], [1, 128]])
                tok_w1 = b.op("sync", lambda e, wo=wslot, src=srcw: e.dma_start(
                    out=bass.AP(ar, wo, [[AF, 128], [128, 8], [1, 128]]), in_=src),
                    waits=wtd, inc="dw", inck=16)
                bk = 2 + (ht % 2)
                if ht < 2:
                    wt15 = [tok_w1, tln2] if e_i == 0 else [tok_w1, acc_tok[6 + ht]]
                else:
                    wt15 = [tok_w1, w1_ev[ht % 2]]
                for kc in range(NKC):
                    b.op("tensor", lambda e, wo=wslot, kc=kc, bk=bk: e.matmul(
                        pb[bk][:, :], arp(wo + kc * 128, 128), arp(o_Xh + kc * 512, 512),
                        start=(kc == 0), stop=(kc == NKC - 1)),
                        waits=(wt15 if kc == 0 else []),
                        inc=("tns" if kc == NKC - 1 else None))
                w1_toks[ht % 2] = b.tok("tns")
                b.op("scalar", lambda e, ht=ht, bk=bk, e_i=e_i: e.activation(
                    arp(o_hdn + ht * 512, 512), pb[bk][:, :], ACTF.Silu,
                    bias=arp(o_rb1 + e_i * 4 + ht, 1), scale=1.0),
                    waits=[w1_toks[ht % 2], hdn_free if ht == 0 else None], inc="act")
                w1_ev[ht % 2] = b.tok("act")
            thdn = b.tok("act")
            if e_i < E:
                for ht in range(4):
                    wt16 = [thdn, cfb_ev] if ht == 0 else []
                    b.op("vector", lambda e, ht=ht: e.tensor_tensor(
                        arp(o_hdn + ht * 512, 512), arp(o_hdn + ht * 512, 512),
                        arp(o_cfb, 512), ALU.mult), waits=wt16,
                        inc=("dve" if ht == 3 else None))
                thdns = b.tok("dve")
                cfb_cons = thdns
            else:
                thdns = thdn
            for ot in range(NKC):
                wslot2 = o_w2r + (ot % 2) * 512
                wtd2 = [w2_toks[ot % 2]] if (e_i > 0 or ot >= 2) else [t_inproj_mm]
                if e_i < E:
                    srcw2 = bass.AP(rw2, e_i * HD * DM + ot * 128, [[DM, 128], [128 * DM, 4], [1, 128]])
                else:
                    srcw2 = bass.AP(sw2, ot * 128, [[DM, 128], [128 * DM, 4], [1, 128]])
                tok_w2 = b.op("sync", lambda e, wo=wslot2, src=srcw2: e.dma_start(
                    out=bass.AP(ar, wo, [[AF, 128], [128, 4], [1, 128]]), in_=src),
                    waits=wtd2, inc="dw", inck=16)
                bk = 2 + (ot % 2)
                wt18 = [tok_w2, thdns, acc_tok[ot] if e_i > 0 else acc_free_tok]
                if ot >= 2:
                    wt18.append(acc_tok[ot - 2])
                for hc in range(4):
                    b.op("tensor", lambda e, wo=wslot2, hc=hc, bk=bk: e.matmul(
                        pb[bk][:, :], arp(wo + hc * 128, 128), arp(o_hdn + hc * 512, 512),
                        start=(hc == 0), stop=(hc == 3)),
                        waits=(wt18 if hc == 0 else []),
                        inc=("tns" if hc == 3 else None))
                w2_toks[ot % 2] = b.tok("tns")
                tmm2 = w2_toks[ot % 2]
                if e_i == 0:
                    b.op("vector", lambda e, ot=ot, bk=bk: e.tensor_copy(
                        arp(acc_off(ot), 512), pb[bk][:, :]), waits=[tmm2], inc="dve")
                else:
                    b.op("vector", lambda e, ot=ot, bk=bk: e.tensor_tensor(
                        arp(acc_off(ot), 512), arp(acc_off(ot), 512),
                        pb[bk][:, :], ALU.add), waits=[tmm2], inc="dve")
                acc_tok[ot] = b.tok("dve")
            hdn_free = w2_toks[1]
            marks.append(len(b.capm))
        thdn_dead = b.tok("tns")

        bchook(4)
        rb2_toks = [None, None]
        xm_toks = [None, None]
        for ot in range(NKC):
            so = o_rb2s
            dwt = [rb2_toks[(ot + 1) % 2]] if ot >= 1 else [fst.get("rb2s_free")]
            tok_b2 = b.op("sync", lambda e, ot=ot, so=so: e.dma_start(
                out=arp(so, 128, 0, 8),
                in_=bass.AP(rb2, ot * 128, [[DM, 8], [1, 128], [1, 1]])),
                waits=dwt, inc="dw", inck=16)
            bk = 2 + (ot % 2)
            wt19 = [tok_b2, tcf, acc_tok[ot]]
            wt19.append(acc_tok[6 + ot] if ot < 2 else rb2_ev[ot % 2])
            b.op("tensor", lambda e, bk=bk, so=so: e.matmul(
                pb[bk][:, :], arp(so, 128, 0, 8), arp(o_cf, 512, 0, 8),
                start=True, stop=True), waits=wt19, inc="tns")
            tb2 = b.tok("tns")
            rb2_toks[ot % 2] = tb2
            xslot = o_xmres + (ot % 2) * 512
            tok_xm = b.op("gpsimd", lambda e, fc=fc, ot=ot, xs=xslot: e.dma_start(
                out=arp(xs, 512),
                in_=bass.AP(x_mid, ot * 128 * L + fc * TC, [[L, 128], [1, TC], [1, 1]])),
                waits=([thdn_dead, t_xmst] if ot < 2 else [xm_toks[ot % 2]]), inc="dg", inck=16)
            b.op("vector", lambda e, ot=ot, bk=bk: e.tensor_tensor(
                arp(acc_off(ot), 512), arp(acc_off(ot), 512), pb[bk][:, :], ALU.add),
                waits=[tb2], inc="dve")
            rb2_ev[ot % 2] = b.tok("dve")
            b.op("vector", lambda e, ot=ot: e.tensor_scalar(
                arp(acc_off(ot), 512), arp(acc_off(ot), 512),
                arp(o_sb2 + ot, 1), None, ALU.add))
            b.op("vector", lambda e, ot=ot, xs=xslot: e.scalar_tensor_tensor(
                arp(acc_off(ot), 512), arp(acc_off(ot), 512), modcol("gb", ot),
                arp(xs, 512), ALU.mult, ALU.add), waits=[tok_xm], inc="dve")
            xm_toks[ot % 2] = b.tok("dve")
        tfin = b.tok("dve")

        ev_prev = None
        stok = None
        for tt in range(4):
            bk = 6 if tt % 2 == 0 else 7
            wt20 = [tfin, thdn_dead] if tt == 0 else [ev_prev]
            for dc in range(4):
                b.op("tensor", lambda e, tt=tt, dc=dc, bk=bk: e.transpose(
                    pb[bk][:, dc * 128:(dc + 1) * 128],
                    arp(acc_off(dc) + tt * 128, 128),
                    arp(o_eye, 128)),
                    waits=(wt20 if dc == 0 else []),
                    inc=("tns" if dc == 3 else None))
            tt1 = b.tok("tns")
            b.op("scalar", lambda e, bk=bk: e.copy(
                arp(o_outst, 512), pb[bk][:, :]), waits=[tt1], inc="act")
            tev1 = b.tok("act")
            for dc in range(4, 8):
                b.op("tensor", lambda e, tt=tt, dc=dc, bk=bk: e.transpose(
                    pb[bk][:, (dc - 4) * 128:(dc - 3) * 128],
                    arp(acc_off(dc) + tt * 128, 128),
                    arp(o_eye, 128)),
                    waits=([tev1] if dc == 4 else []),
                    inc=("tns" if dc == 7 else None))
            tt2 = b.tok("tns")
            b.op("scalar", lambda e, bk=bk: e.copy(
                arp(o_outst + 512, 512), pb[bk][:, :]), waits=[tt2], inc="act")
            tev = b.tok("act")
            stok = b.op("scalar", lambda e, fc=fc, tt=tt: e.dma_start(
                out=bass.AP(out, (fc * TC + tt * 128) * DM, [[DM, 128], [1, DM], [1, 1]]),
                in_=bass.AP(ar, o_outst, [[AF, 128], [1, 1024], [1, 1]])),
                waits=[tev], inc="st", inck=16)
            ev_prev = stok
        caps = b.end_capture()
        fst["rb2s_free"] = rb2_toks[0]
        return dict(caps=caps, caps_front=caps_front, marks=marks, store_last=stok,
                    tensor_done=b.tok("tns"),
                    dve_done=tfin, thdn_dead=thdn_dead, w1_last=w1_toks, w2_last=w2_toks)

    o_xtok = o_big
    o_xi = o_big
    o_y = o_big

    st_total = 16 * 4 * NCH
    tprev_tensor = None
    tprev_store = None
    tprev_gate = None
    tprev_tailsave = None
    last_state_tok = [None, None]
    hc_tok = [None, None]
    prev_f = None
    prev_f_caps = None

    for ch in range(NCH):
        wt = [tmodv] if ch == 0 else [tprev_tensor, tprev_store]
        tok_x = b.op("scalar", lambda e, ch=ch: e.dma_start(
            out=bass.AP(ar, o_xtok, [[AF, 128], [1024, 4], [1, 1024]]),
            in_=bass.AP(x_in, ch * TC * DM, [[DM, 128], [128 * DM, 4], [1, DM]])),
            waits=wt, inc="dx", inck=16)
        for dc in range(NKC):
            bk = dc % 2
            wt2 = [tok_x, t_eye] if dc == 0 else []
            if dc >= 2:
                wt2.append(("act", b.cnt["act"] - 1))
            for tt in range(4):
                b.op("tensor", lambda e, dc=dc, tt=tt, bk=bk: e.transpose(
                    pb[bk][:, tt * 128:(tt + 1) * 128],
                    arp(o_xtok + tt * 1024 + dc * 128, 128),
                    arp(o_eye, 128)),
                    waits=(wt2 if tt == 0 else []),
                    inc=("tns" if tt == 3 else None))
            ttr = b.tok("tns")
            b.op("scalar", lambda e, dc=dc, bk=bk: e.copy(
                arp(o_xT + dc * 512, 512), pb[bk][:, :]), waits=[ttr], inc="act")
        txT = b.tok("act")

        tln = emit_ln(o_xT, o_hT, "ca1", "sa", [txT],
                      o_R1, o_R2, o_rows, o_rows + 512, o_rows + 1024, o_R1, o_R2)

        evac_toks = []
        ip_toks = [None, None]
        for jt in range(NJT):
            woff = o_wring + (jt % 2) * 1024
            if ch == 0 and jt < 2:
                wt3 = [t_mod_mm]
            elif jt < 2:
                wt3 = [tprev_tensor]
            else:
                wt3 = [ip_toks[jt % 2]]
            tok_wA = b.op("sync", lambda e, jt=jt, wo=woff: e.dma_start(
                out=bass.AP(ar, wo, [[AF, 128], [128, 4], [1, 128]]),
                in_=bass.AP(w_inp, jt * 128, [[4 * DM, 128], [128 * 4 * DM, 4], [1, 128]])),
                waits=wt3, inc="dw", inck=16)
            tok_w = b.op("gpsimd", lambda e, jt=jt, wo=woff: e.dma_start(
                out=bass.AP(ar, wo + 512, [[AF, 128], [128, 4], [1, 128]]),
                in_=bass.AP(w_inp, jt * 128 + 4 * 128 * 4 * DM, [[4 * DM, 128], [128 * 4 * DM, 4], [1, 128]])),
                waits=wt3, inc="dg", inck=16)
            bk = 2 + (jt % 2)
            wt4 = [tok_wA, tln]
            if jt >= 2:
                wt4.append(("act", b.cnt["act"] - 1))
            for kc in range(NKC):
                b.op("tensor", lambda e, wo=woff, kc=kc, bk=bk: e.matmul(
                    pb[bk][:, :], arp(wo + kc * 128, 128), arp(o_hT + kc * 512, 512),
                    start=(kc == 0), stop=(kc == NKC - 1)),
                    waits=(wt4 if kc == 0 else ([tok_w] if kc == 4 else [])),
                    inc=("tns" if kc == NKC - 1 else None))
            ip_toks[jt % 2] = b.tok("tns")
            tmm = ip_toks[jt % 2]
            if jt < 16:
                wt5 = [tmm]
                if ch > 0 and jt == 0:
                    wt5.append(tprev_tailsave)
                b.op("scalar", lambda e, jt=jt, bk=bk: e.copy(
                    arp(o_xi + jt * 515 + 3, 512), pb[bk][:, :]), waits=wt5, inc="act")
            else:
                wt5 = [tmm]
                if jt == 16 and prev_f is not None:
                    wt5.append(prev_f["store_last"])
                b.op("scalar", lambda e, jt=jt, bk=bk: e.activation(
                    arp(o_sz + (jt - 16) * 512, 512), pb[bk][:, :], ACTF.Silu),
                    waits=wt5, inc="act")
            evac_toks.append(b.tok("act"))
        t_ip_mm = b.tok("tns")
        tsz_done = b.tok("act")
        t_spill = b.op("gpsimd", lambda e, ch=ch: e.dma_start(
            out=bass.AP(sz_sp, 0, [[TC, 128], [128 * TC, 16], [1, TC]]),
            in_=bass.AP(ar, o_sz, [[AF, 128], [512, 16], [1, 512]])),
            waits=[tsz_done, tprev_gate], inc="dg", inck=16)

        for cc in range(NCC):
            base = o_xi + cc * 515
            wtc = [evac_toks[cc]]
            if cc == 0 and prev_f is not None:
                wtc.append(prev_f["dve_done"])
            b.op("vector", lambda e, cc=cc, base=base: e.tensor_copy(
                arp(base, 3), arp(o_tails + cc * 3, 3)), waits=wtc)
            b.op("vector", lambda e, cc=cc, base=base: e.tensor_scalar(
                arp(o_xc + cc * 512, 512), arp(base, 512), arp(o_convw + cc * 4, 1), None, ALU.mult))
            for tau in range(1, 4):
                b.op("vector", lambda e, cc=cc, base=base, tau=tau: e.scalar_tensor_tensor(
                    arp(o_xc + cc * 512, 512), arp(base + tau, 512), arp(o_convw + cc * 4 + tau, 1),
                    arp(o_xc + cc * 512, 512), ALU.mult, ALU.add),
                    inc=("dve" if tau == 3 else None))
            tcv = b.tok("dve")
            b.op("scalar", lambda e, cc=cc: e.activation(
                arp(o_xc + cc * 512, 512), arp(o_xc + cc * 512, 512), ACTF.Silu,
                bias=arp(o_convb + cc, 1), scale=1.0), waits=[tcv], inc="act")
        tconv = b.tok("act")
        for cc in range(NCC):
            b.op("scalar", lambda e, cc=cc: e.copy(
                arp(o_tails + cc * 3, 3), arp(o_xi + cc * 515 + 512, 3)), inc="act")
        tprev_tailsave = b.tok("act")

        for cc in range(NCC):
            wty = [tconv, tprev_tailsave] if cc == 0 else []
            b.op("vector", lambda e, cc=cc: e.tensor_scalar(
                arp(o_y + cc * 512, 512), arp(o_xc + cc * 512, 512),
                arp(o_dvec + cc, 1), None, ALU.mult), waits=wty,
                inc=("dve" if cc == NCC - 1 else None))
        tyinit = b.tok("dve")

        for cc in range(NCC):
            wt7 = [tconv, t_xp] if cc == 0 else []
            b.op("tensor", lambda e, cc=cc: e.matmul(
                pb[4][0:33, :], arp(o_xpw + cc * 33, 33), arp(o_xc + cc * 512, 512),
                start=(cc == 0), stop=(cc == NCC - 1)),
                waits=wt7, inc=("tns" if cc == NCC - 1 else None))
        txpj = b.tok("tns")
        b.op("scalar", lambda e: e.copy(arp(o_bcdt, 512, 0, 33), pb[4][0:33, :]),
             waits=[txpj], inc="act")
        tbcdt = b.tok("act")
        b.op("tensor", lambda e: e.matmul(pb[5][:, :], arp(o_ones, 128, 0, 1),
                                          arp(o_bcdt, 512, 0, 1), start=True, stop=True),
             waits=[tbcdt], inc="tns")
        tdrawmm = b.tok("tns")
        b.op("scalar", lambda e: e.copy(arp(o_scr2, 512), pb[5][:, :]), waits=[tdrawmm], inc="act")
        tdraw = b.tok("act")

        for g2 in range(16):
            dst = o_hT + (g2 % 8) * 512 if g2 < 8 else o_mdt + (g2 - 8) * 512
            wt8 = [tdraw, t_ip_mm] if g2 == 0 else []
            b.op("scalar", lambda e, g2=g2, dst=dst: e.activation(
                arp(dst, 512), arp(o_scr2, 512), ACTF.Sigmoid,
                bias=arp(o_ndtb + g2, 1), scale=arp(o_ndtw + g2, 1)), waits=wt8)
        for g2 in range(16):
            dst = o_hT + (g2 % 8) * 512 if g2 < 8 else o_mdt + (g2 - 8) * 512
            b.op("scalar", lambda e, g2=g2, dst=dst: e.activation(
                arp(dst, 512), arp(dst, 512), ACTF.Ln), inc="act")
        tmdt = b.tok("act")
        for g2 in range(16):
            src = o_hT + (g2 % 8) * 512 if g2 < 8 else o_mdt + (g2 - 8) * 512
            wt9 = [tmdt, tyinit] if g2 == 0 else []
            b.op("vector", lambda e, g2=g2, src=src: e.tensor_tensor(
                arp(o_dtxm + g2 * 512, 512), arp(src, 512),
                arp(o_dtxm + g2 * 512, 512), ALU.mult), waits=wt9,
                inc=("dve" if g2 == 15 else None))
        tdtx = b.tok("dve")

        bc_emitted = [0]
        bc_mm_toks = [None] * 32
        rowdma_toks = [None] * 32
        prev_row_mm = [None, None]
        bc_base = 32 * ch

        def emit_pair(k):
            s = k % 16
            slB = o_rows if k % 2 == 0 else o_rows + 512
            slC = (o_rows + 1024) if k % 2 == 0 else o_R1
            dwt = [tbcdt, prev_row_mm[k % 2]]
            tokB = b.op("gpsimd", lambda e, s=s, sl=slB: e.dma_start(
                out=arp(sl, 512, 0, 1), in_=ar[1 + s:2 + s, o_bcdt:o_bcdt + 512]),
                waits=dwt, inc="dg", inck=16)
            tokC = b.op("gpsimd", lambda e, s=s, sl=slC: e.dma_start(
                out=arp(sl, 512, 0, 1), in_=ar[17 + s:18 + s, o_bcdt:o_bcdt + 512]),
                inc="dg", inck=16)
            rowdma_toks[k] = tokC
            bkB = 4 if k % 2 == 0 else 0
            bkC = 5 if k % 2 == 0 else 1
            b.op("tensor", lambda e, sl=slB, bkB=bkB, k=k: e.matmul(
                pb[bkB][:, :], arp(o_ones, 128, 0, 1),
                arp(sl, 512, 0, 1), start=True, stop=True),
                waits=[tokC, ("bc", bc_base + max(0, k - 1))] + ([tdraw] if k == 0 else []))
            b.op("tensor", lambda e, sl=slC, bkC=bkC: e.matmul(
                pb[bkC][:, :], arp(o_ones, 128, 0, 1),
                arp(sl, 512, 0, 1), start=True, stop=True), inc="tns")
            bc_mm_toks[k] = b.tok("tns")
            prev_row_mm[k % 2] = bc_mm_toks[k]
            bc_emitted[0] += 1

        def bchook(n=4):
            for _ in range(n):
                if bc_emitted[0] < 32:
                    emit_pair(bc_emitted[0])

        if ch > 0:
            acc_free = prev_f["tensor_done"] if prev_f is not None else None
            f_res = emit_F(ch - 1, t_spill, tprev_store, t_ip_mm, acc_free, bchook)
        else:
            f_res = None

        if f_res is not None:
            b.drain(f_res["caps_front"])
        fcaps = f_res["caps"] if f_res is not None else []
        dtgt = [0] * 32
        if f_res is not None:
            mk = f_res["marks"]
            prev = 0
            for j in range(E + 1):
                cnt = mk[j] - prev
                prev = mk[j]
                base_i = 3 * j + 4 if j < E else 28
                q, r = divmod(cnt, 3)
                for ii in range(3):
                    dtgt[base_i + ii] += q + (1 if ii < r else 0)
            rest = len(fcaps) - mk[E]
            q, r = divmod(rest, 3)
            for ii in range(3):
                dtgt[29 + ii] += q + (1 if ii < r else 0)
        bc_cons = [None, None]
        for dh in range(2):
            for s in range(S):
                k = dh * 16 + s
                while bc_emitted[0] <= k:
                    emit_pair(bc_emitted[0])
                sl = k % 2
                bkB = 4 if sl == 0 else 0
                bkC = 5 if sl == 0 else 1
                b.op("scalar", lambda e, sl=sl, bkB=bkB: e.copy(
                    arp(o_Bbc + sl * 512, 512), pb[bkB][:, :]),
                    waits=[bc_mm_toks[k], bc_cons[sl]])
                b.op("scalar", lambda e, sl=sl, bkC=bkC: e.copy(
                    arp(o_Cbc + sl * 512, 512), pb[bkC][:, :]), inc="act")
                b.op("scalar", lambda e: e.copy(arp(o_lgs + 50, 1), arp(o_lgs + 50, 1)),
                     inc="bc")
                tBC = b.tok("act")
                for g in range(8):
                    ga_ = dh * 8 + g
                    slot = g % 2
                    msrc = (o_hT if dh == 0 else o_mdt) + g * 512
                    wtA = [hc_tok[slot], last_state_tok[slot]]
                    b.op("scalar", lambda e, msrc=msrc, ga_=ga_, s=s, slot=slot: e.activation(
                        arp(o_dAr + slot * 512, 512), arp(msrc, 512), ACTF.Exp,
                        scale=arp(o_nega + ga_ * 16 + s, 1)), waits=wtA, inc="act")
                    tda = b.tok("act")
                    wtB = [tda] + ([tBC, tdtx] if g == 0 else [])
                    b.op("vector", lambda e, ga_=ga_, slot=slot, sl=sl: e.tensor_tensor(
                        arp(o_dBr + slot * 512, 512), arp(o_dtxm + ga_ * 512, 512),
                        arp(o_Bbc + sl * 512, 512), ALU.mult), waits=wtB, inc="dve")
                    b.op("vector", lambda e, ga_=ga_, s=s, slot=slot: e.tensor_tensor_scan(
                        arp(o_hr + slot * 512, 512), arp(o_dAr + slot * 512, 512),
                        arp(o_dBr + slot * 512, 512), arp(o_state + ga_ * 16 + s, 1),
                        ALU.mult, ALU.subtract), inc="dve")
                    tscan = b.tok("dve")
                    b.op("scalar", lambda e, ga_=ga_, s=s, slot=slot: e.copy(
                        arp(o_state + ga_ * 16 + s, 1), arp(o_hr + slot * 512 + 511, 1)),
                        waits=[tscan], inc="act")
                    last_state_tok[slot] = b.tok("act")
                    b.op("vector", lambda e, slot=slot, sl=sl: e.tensor_tensor(
                        arp(o_dBr + slot * 512, 512), arp(o_hr + slot * 512, 512),
                        arp(o_Cbc + sl * 512, 512), ALU.mult), inc="dve")
                    hc_tok[slot] = b.tok("dve")
                    b.op("vector", lambda e, ga_=ga_, slot=slot: e.tensor_tensor(
                        arp(o_y + ga_ * 512, 512), arp(o_y + ga_ * 512, 512),
                        arp(o_dBr + slot * 512, 512), ALU.add), inc="dve")
                bc_cons[sl] = b.tok("dve")
                b.drain(fcaps, dtgt[k])
        b.drain(fcaps)
        tscan_all = b.tok("dve")

        gate_dma = [None, None]
        for cc in range(NCC):
            gsl = o_rows if cc % 2 == 0 else o_rows + 512
            gwt = [tscan_all if cc < 2 else None, t_spill if cc < 2 else None,
                   gate_dma[cc % 2]]
            gtok = b.op("gpsimd", lambda e, cc=cc, gs=gsl: e.dma_start(
                out=arp(gs, 512),
                in_=bass.AP(sz_sp, cc * 128 * TC, [[TC, 128], [1, TC], [1, 1]])),
                waits=gwt, inc="dg", inck=16)
            wtg = [gtok] + ([tscan_all] if cc == 0 else [])
            b.op("vector", lambda e, cc=cc, gs=gsl: e.tensor_tensor(
                arp(o_y + cc * 512, 512), arp(o_y + cc * 512, 512),
                arp(gs, 512), ALU.mult), waits=wtg, inc="dve")
            gate_dma[cc % 2] = b.tok("dve")
        tgate = b.tok("dve")
        tprev_gate = tgate

        op_toks = [None]
        for ot in range(NKC):
            woff = o_wring
            if ot == 0:
                wt10 = [f_res["thdn_dead"] if f_res is not None else t_ip_mm]
            else:
                wt10 = [op_toks[0]]
            tok_wA = b.op("sync", lambda e, ot=ot, wo=woff: e.dma_start(
                out=bass.AP(ar, wo, [[AF, 128], [128, 8], [1, 128]]),
                in_=bass.AP(w_outp, ot * 128, [[DM, 128], [128 * DM, 8], [1, 128]])),
                waits=wt10, inc="dw", inck=16)
            tok_w = b.op("gpsimd", lambda e, ot=ot, wo=woff: e.dma_start(
                out=bass.AP(ar, wo + 1024, [[AF, 128], [128, 8], [1, 128]]),
                in_=bass.AP(w_outp, ot * 128 + 8 * 128 * DM, [[DM, 128], [128 * DM, 8], [1, 128]])),
                waits=wt10, inc="dg", inck=16)
            bk = 2 + (ot % 2)
            wt11 = [tok_wA, tgate]
            if ot >= 2:
                wt11.append(("dve", b.cnt["dve"] - 1))
            for cc in range(NCC):
                b.op("tensor", lambda e, wo=woff, cc=cc, bk=bk: e.matmul(
                    pb[bk][:, :], arp(wo + cc * 128, 128), arp(o_y + cc * 512, 512),
                    start=(cc == 0), stop=(cc == NCC - 1)),
                    waits=(wt11 if cc == 0 else ([tok_w] if cc == 8 else [])),
                    inc=("tns" if cc == NCC - 1 else None))
            op_toks[0] = b.tok("tns")
            tmo = op_toks[0]
            b.op("vector", lambda e, ot=ot, bk=bk: e.scalar_tensor_tensor(
                arp(o_xT + ot * 512, 512), pb[bk][:, :], modcol("ga", ot),
                arp(o_xT + ot * 512, 512), ALU.mult, ALU.add), waits=[tmo], inc="dve")
        tresid = b.tok("dve")
        tprev_tensor = b.tok("tns")
        tprev_store = b.op("gpsimd", lambda e, ch=ch: e.dma_start(
            out=bass.AP(x_mid, ch * TC, [[L, 128], [128 * L, 8], [1, TC]]),
            in_=bass.AP(ar, o_xT, [[AF, 128], [512, 8], [1, 512]])),
            waits=[tresid], inc="dg", inck=16)
        prev_f = f_res

    f_res = emit_F(NCH - 1,
                   prev_f["store_last"] if prev_f is not None else tprev_store,
                   tprev_store,
                   tprev_tensor,
                   prev_f["tensor_done"] if prev_f is not None else None,
                   lambda n=4: None)
    b.drain(f_res["caps_front"])
    b.drain(f_res["caps"])

    b.waits("gpsimd", [("st", st_total)])

    with nc.Block() as block:
        @block.sync
        def _(e):
            for f in b.qs["sync"]:
                f(e)

        @block.scalar
        def _(e):
            for f in b.qs["scalar"]:
                f(e)

        @block.vector
        def _(e):
            for f in b.qs["vector"]:
                f(e)

        @block.tensor
        def _(e):
            for f in b.qs["tensor"]:
                f(e)

        @block.gpsimd
        def _(e):
            for f in b.qs["gpsimd"]:
                f(e)

    stack.close()
    return nc


_NC_CACHE = None


def kernel(**inputs):
    global _NC_CACHE
    if _NC_CACHE is None:
        _NC_CACHE = build_program()
    nc = _NC_CACHE
    x = np.ascontiguousarray(inputs["L0_attn_x"], dtype=np.float32)
    base = {
        "in_proj_w": np.ascontiguousarray(inputs["L0_attn_in_proj_w"], np.float32),
        "conv_w": np.ascontiguousarray(inputs["L0_attn_conv_w"], np.float32),
        "conv_b": np.ascontiguousarray(inputs["L0_attn_conv_b"], np.float32),
        "x_proj_w": np.ascontiguousarray(inputs["L0_attn_x_proj_w"], np.float32),
        "dt_w": np.ascontiguousarray(inputs["L0_attn_dt_w"], np.float32),
        "dt_b": np.ascontiguousarray(inputs["L0_attn_dt_b"], np.float32),
        "A_log": np.ascontiguousarray(inputs["L0_attn_A_log"], np.float32),
        "D": np.ascontiguousarray(inputs["L0_attn_D"], np.float32),
        "out_proj_w": np.ascontiguousarray(inputs["L0_attn_out_proj_w"], np.float32),
        "routing_W": np.ascontiguousarray(inputs["L0_ffn_routing_W"], np.float32),
        "expert_bias": np.ascontiguousarray(inputs["L0_ffn_expert_bias"], np.float32),
        "sW1": np.ascontiguousarray(inputs["L0_ffn_sW1"], np.float32),
        "sb1": np.ascontiguousarray(inputs["L0_ffn_sb1"], np.float32),
        "sW2": np.ascontiguousarray(inputs["L0_ffn_sW2"], np.float32),
        "sb2": np.ascontiguousarray(inputs["L0_ffn_sb2"], np.float32),
        "rW1": np.ascontiguousarray(inputs["L0_ffn_rW1"], np.float32),
        "rb1": np.ascontiguousarray(inputs["L0_ffn_rb1"], np.float32),
        "rW2": np.ascontiguousarray(inputs["L0_ffn_rW2"], np.float32),
        "rb2": np.ascontiguousarray(inputs["L0_ffn_rb2"], np.float32),
        "wmod": np.ascontiguousarray(inputs["L0_wmod"], np.float32),
        "eye": np.eye(128, dtype=np.float32),
        "onesrow": np.ones((1, 128), np.float32),
    }
    cs = np.ascontiguousarray(inputs["L0_c"], np.float32)
    in_maps = []
    for b_i in range(8):
        m = dict(base)
        m["x"] = x[b_i]
        m["c_vec"] = cs[b_i]
        in_maps.append(m)
    res = run_bass_kernel_spmd(nc, in_maps, core_ids=list(range(8)))
    return np.stack([r["out"] for r in res.results], axis=0)


if __name__ == "__main__":
    build_program()
    print("built ok")
